# version 31
# baseline (speedup 1.0000x reference)
import math
import numpy as np
import ml_dtypes
import concourse.bass as bass
import concourse.mybir as mybir
from concourse.bass_utils import run_bass_kernel_spmd

F32 = mybir.dt.float32
BF16 = mybir.dt.bfloat16
FP8 = mybir.dt.float8e4
I32 = mybir.dt.int32
AF = mybir.ActivationFunctionType
ALU = mybir.AluOpType
AX = mybir.AxisListType

NCORE = 8
L, D = 4096, 1024
H, E = 8, 128
HALF = 64
DH = 4096
C = 256
NCH = L // C
TT = C // 128
JC = DH // 128
DC = D // 128
HS = 258
EPS = 1e-5
SCALE_BASE = 512.0

ENGS = ("tensor", "vector", "scalar", "gpsimd", "sync")


def _host_tables():
    t = {}
    perm = np.concatenate([np.arange(0, E, 2), np.arange(1, E, 2)])
    t["perm"] = perm
    pos = np.arange(L, dtype=np.float64)
    pairidx = np.arange(HALF, dtype=np.float64)
    base = (2 * pairidx + 0.4 * E) / (1.4 * E)
    inv_freq = 1.0 / 10000.0 ** (pairidx / HALF)
    sinus = pos[:, None] * inv_freq[None, :]
    sin, cos = np.sin(sinus), np.cos(sinus)
    i_rel = (np.arange(L) % C).astype(np.float64)
    scl = base[None, :] ** (i_rel[:, None] / SCALE_BASE)
    cq = np.concatenate([cos, cos], 1) * np.concatenate([scl, scl], 1)
    sq = np.concatenate([-sin, sin], 1) * np.concatenate([scl, scl], 1)
    ck = np.concatenate([cos, cos], 1) / np.concatenate([scl, scl], 1)
    sk = np.concatenate([-sin, sin], 1) / np.concatenate([scl, scl], 1)

    def tile_tab(a):
        a = a.reshape(NCH, TT, 128, E).transpose(0, 2, 1, 3).reshape(NCH, 128, TT * E)
        return np.ascontiguousarray(a.astype(np.float32))
    t["cq"], t["sq"], t["ck"], t["sk"] = map(tile_tab, (cq, sq, ck, sk))

    gam = 1.0 - np.exp(np.linspace(math.log(1 / 32), math.log(1 / 512), H))
    i = np.arange(C, dtype=np.float64)

    def colpack(a):
        return np.ascontiguousarray(
            a.reshape(TT, 128, H).transpose(1, 0, 2).reshape(128, TT * H).astype(np.float32))
    t["g_i"] = colpack(gam[None, :] ** i[:, None])
    t["g_negm"] = colpack(gam[None, :] ** (-i[:, None]))
    t["g_dec"] = colpack(gam[None, :] ** (C - 1.0 - i[:, None]))
    tri = (np.arange(C)[None, :] >= np.arange(C)[:, None]).astype(np.float32)
    t["tri0"] = np.ascontiguousarray(tri[0:128, :])
    t["tri1"] = np.ascontiguousarray(tri[128:256, 128:256])
    scC = np.concatenate([base, base]) ** (C / SCALE_BASE)
    t["svec"] = np.ascontiguousarray((scC[:, None] * gam[None, :] ** C).astype(np.float32))
    t["uvec"] = np.ascontiguousarray((scC[:, None] * gam[None, :] ** 1).astype(np.float32))
    t["ident"] = np.eye(128, dtype=ml_dtypes.bfloat16)
    t["identf"] = np.eye(128, dtype=np.float32)
    t["ones_row"] = np.ones((1, 128), dtype=np.float32)
    t["ones_col"] = np.ones((128, 1), dtype=np.float32)
    return t


def _pack_dxk(w, ncols):
    a = np.asarray(w, np.float32).reshape(DC, 128, ncols).transpose(1, 0, 2)
    return np.ascontiguousarray(a.reshape(128, DC * ncols))


def host_prep(inputs):
    t = _host_tables()
    perm = t["perm"]
    WQ = np.transpose(np.asarray(inputs["L0_attn_W_Q"], np.float32)[:, :, perm], (1, 0, 2)).reshape(D, H * E)
    WK = np.transpose(np.asarray(inputs["L0_attn_W_K"], np.float32)[:, :, perm], (1, 0, 2)).reshape(D, H * E)
    WV = np.transpose(np.asarray(inputs["L0_attn_W_V"], np.float32), (1, 0, 2)).reshape(D, H * E)
    shared = {
        "wq_p": _pack_dxk(WQ, 1024),
        "wk_p": _pack_dxk(WK, 1024),
        "wv_p": _pack_dxk(WV, 1024),
        "wg_p": _pack_dxk(np.asarray(inputs["L0_attn_W_G"], np.float32), 1024),
        "wo_p": _pack_dxk(np.asarray(inputs["L0_attn_W_O"], np.float32), 1024),
        "w1_p": _pack_dxk(np.asarray(inputs["L0_ffn_w1"], np.float32).T, DH),
        "w2_p": np.ascontiguousarray(
            np.asarray(inputs["L0_ffn_w2"], np.float32).T.reshape(JC, 128, D)
            .transpose(1, 0, 2).reshape(128, JC * D)),
        "convw": np.ascontiguousarray(
            np.asarray(inputs["L0_ffn_conv_w"], np.float32)[:, 0, :].reshape(JC, 128, 3)
            .transpose(1, 0, 2).reshape(128, JC * 3)),
        "convb": np.ascontiguousarray(
            np.asarray(inputs["L0_ffn_conv_b"], np.float32).reshape(JC, 128).T),
        "gnw_c": np.ascontiguousarray(
            np.asarray(inputs["L0_attn_gn_weight"], np.float32).reshape(DC, 128).T),
        "gnb_c": np.ascontiguousarray(
            np.asarray(inputs["L0_attn_gn_bias"], np.float32).reshape(DC, 128).T),
        "cq": t["cq"], "sq": t["sq"], "ck": t["ck"], "sk": t["sk"],
        "g_i": t["g_i"], "g_negm": t["g_negm"], "g_dec": t["g_dec"],
        "tri0": t["tri0"], "tri1": t["tri1"], "svec": t["svec"], "uvec": t["uvec"],
        "ident": t["ident"], "identf": t["identf"],
        "ones_row": t["ones_row"], "ones_col": t["ones_col"],
        "kconsts": np.broadcast_to(np.array([1e-5, 16384.0], np.float32), (128, 2)).copy(),
    }
    X = np.asarray(inputs["L0_attn_X"], np.float32)
    return [dict(shared, x_in=np.ascontiguousarray(X[b])) for b in range(NCORE)]


NDL = 6
SAME_ENGINE_SKIP = False
SEMS = ENGS + ("gdma",) + tuple(f"dsync{i}" for i in range(NDL))


class Sched:
    def __init__(self):
        self.ops = {e: [] for e in ENGS}
        self.count = {s: 0 for s in SEMS}
        self.know = {e: {s: 0 for s in SEMS} for e in ENGS}
        self.snaps = {s: [(0, {q: 0 for q in SEMS})] for s in SEMS}
        self.last_w = {}
        self.readers = {}

    ALIAS = {"pT": "pA", "pR": "pB"}
    BUFOF = (("xn1f", "bfB"), ("gat", "bfB"), ("yhf", "bfA"), ("a1f", "bfA"),
             ("r2q", "tmB"), ("kxtm", "tmB"), ("qgtm", "tmA"), ("stm", "tmA"),
             ("rowsb", "workE"), ("xn1tm", "tmA"), ("row0", "workE"), ("row1", "workE"),
             ("row2", "workE"), ("row3", "workE"), ("mxq", "workE"),
             ("sm1", "kxf"))

    def _expand(self, keys):
        out = []
        for k in keys:
            k = self.ALIAS.get(k, k)
            out.append(k)
            for pre, buf in self.BUFOF:
                if k.startswith(pre):
                    out.append(buf)
                    break
        return tuple(out)

    def op(self, eng, fn, reads=(), writes=(), rr=(), inc=1, sem=None):
        semn = sem or eng
        reads = self._expand(tuple(reads) + tuple(rr))
        writes = self._expand(writes)
        deps = []
        for k in reads + tuple(writes):
            tok = self.last_w.get(k)
            if tok is not None:
                deps.append(tok)
        for k in writes:
            deps.extend(self.readers.get(k, ()))
        waits = []
        kn = self.know[eng]
        for (pe, pv) in deps:
            if kn[pe] >= pv:
                continue
            if SAME_ENGINE_SKIP and pe == eng and not pe.startswith("d") and pe != "gdma":
                continue
            waits.append((pe, pv))
            kn[pe] = max(kn[pe], pv)
            snaps = self.snaps[pe]
            lo, hi, best = 0, len(snaps) - 1, None
            while lo <= hi:
                mid = (lo + hi) // 2
                if snaps[mid][0] <= pv:
                    best = snaps[mid][1]
                    lo = mid + 1
                else:
                    hi = mid - 1
            if best:
                for q, qv in best.items():
                    if qv > kn[q]:
                        kn[q] = qv
        self.count[semn] += inc
        tok = (semn, self.count[semn])
        if inc:
            self.snaps[semn].append((self.count[semn], dict(kn)))
        self.ops[eng].append((waits, fn, inc, semn))
        for k in reads:
            self.readers.setdefault(k, []).append(tok)
        for k in writes:
            self.last_w[k] = tok
            self.readers[k] = []
        return tok

    def emit_all(self, nc, sems):
        with nc.Block() as block:
            for eng in ENGS:
                ops = self.ops[eng]
                if not ops:
                    continue

                def body(engobj, ops=ops, eng=eng):
                    for (waits, fn, inc, semn) in ops:
                        for (pe, pv) in waits:
                            engobj.wait_ge(sems[pe], pv)
                        inst = fn(engobj)
                        if inc and inst is not None:
                            inst.then_inc(sems[semn], inc)
                getattr(block, eng)(body)


SKIP = set()
STAGES = ["p0", "p1", "ln1", "proj", "trans", "g", "scores", "y", "state",
          "gn", "gate", "wo", "ln2", "a1", "mm1", "conv", "q2", "mm2", "all"]


def build_program(n_run=NCH, dumps=(), stage="all"):
    stage_i = STAGES.index(stage)
    def reached(s):
        return STAGES.index(s) > stage_i
    nc = bass.Bass("TRN2", target_bir_lowering=False)
    dumps = {lbl: (shape, dt) for (lbl, shape, dt) in dumps}
    dbg_ap = {}

    def din(name, shape, dt):
        return nc.dram_tensor(name, list(shape), dt, kind="ExternalInput").ap()

    x_in = din("x_in", (L, D), F32)
    wq_p = din("wq_p", (128, DC * 1024), F32)
    wk_p = din("wk_p", (128, DC * 1024), F32)
    wv_p = din("wv_p", (128, DC * 1024), F32)
    wg_p = din("wg_p", (128, DC * 1024), F32)
    wo_p = din("wo_p", (128, DC * 1024), F32)
    w1_p = din("w1_p", (128, DC * DH), F32)
    w2_p = din("w2_p", (128, JC * D), F32)
    convw_d = din("convw", (128, JC * 3), F32)
    convb_d = din("convb", (128, JC), F32)
    gnw_d = din("gnw_c", (128, DC), F32)
    gnb_d = din("gnb_c", (128, DC), F32)
    tab_d = [din(n, (NCH, 128, TT * E), F32) for n in ("cq", "sq", "ck", "sk")]
    gi_d = din("g_i", (128, TT * H), F32)
    gm_d = din("g_negm", (128, TT * H), F32)
    gd_d = din("g_dec", (128, TT * H), F32)
    tri0_d = din("tri0", (128, 256), F32)
    tri1_d = din("tri1", (128, 128), F32)
    sv_d = din("svec", (128, H), F32)
    uv_d = din("uvec", (128, H), F32)
    id_d = din("ident", (128, 128), BF16)
    idf_d = din("identf", (128, 128), F32)
    kc_d = din("kconsts", (128, 2), F32)
    onr_d = din("ones_row", (1, 128), F32)
    onc_d = din("ones_col", (128, 1), F32)

    out_d = nc.dram_tensor("out", [L, D], F32, kind="ExternalOutput").ap()
    wv_s = nc.dram_tensor("wv_s", [128, DC * 1024], BF16, kind="Internal").ap()
    wg_s = nc.dram_tensor("wg_s", [128, DC * 1024], BF16, kind="Internal").ap()
    wo_s = nc.dram_tensor("wo_s", [128, DC * 1024], BF16, kind="Internal").ap()
    r1_s = nc.dram_tensor("r1_s", [2, 128, DC * 2048], FP8, kind="Internal").ap()
    r2_s = nc.dram_tensor("r2_s", [2, 128, 16 * D], FP8, kind="Internal").ap()
    for lbl, (shape, dt) in dumps.items():
        dbg_ap[lbl] = nc.dram_tensor("dbg_" + lbl, list(shape), dt, kind="ExternalOutput").ap()

    from contextlib import ExitStack
    with ExitStack() as stack:
        def sb(name, shape, dt):
            return stack.enter_context(nc.sbuf_tensor(name, list(shape), dt))

        def psal(name, shape, dt):
            return stack.enter_context(nc.psum_tensor(name, list(shape), dt))

        wq = sb("wq", (128, DC * 1024), BF16)
        wk = sb("wk", (128, DC * 1024), BF16)
        strA = sb("strA", (128, 8192), BF16)
        strB = sb("strB", (128, 8192), BF16)
        tabs_t = sb("tabs", (128, 4 * TT * E), F32)
        xs_t = sb("xs", (128, 3 * TT * 1024), F32)
        hbf_t = sb("hbf", (128, 2 * JC * HS), BF16)
        def tabs_ap(buf, i):
            return tabs_t[:, i * TT * E:(i + 1) * TT * E]
        def xs_ap(s):
            return xs_t[:, s * 2048:(s + 1) * 2048]
        def hbf_ap(b):
            return hbf_t[:, b * JC * HS:(b + 1) * JC * HS]
        st2f = sb("st2f", (128, H * E), F32)
        st2b = sb("st2b", (128, H * E), BF16)
        gic = sb("gic", (128, TT * H), F32)
        gmc = sb("gmc", (128, TT * H), F32)
        gdc = sb("gdc", (128, TT * H), F32)
        tri0 = sb("s_tri0", (128, 256), F32)
        svc = sb("svc", (128, H), F32)
        uvc = sb("uvc", (128, H), F32)
        ident = sb("s_id", (128, 128), BF16)
        identf = sb("s_idf", (128, 128), F32)
        kconsts = sb("kconsts_s", (128, 2), F32)
        nc.const_aps.register(F32, 1e-5, kconsts[:, 0:1])
        nc.const_aps.register(F32, 16384.0, kconsts[:, 1:2])
        onesr = sb("onesr", (1, 128), F32)
        onesc = sb("onesc", (128, 1), F32)
        cwc = sb("cwc", (128, JC * 3), F32)
        cbc = sb("cbc", (128, JC), F32)
        gnwc = sb("gnwc", (128, DC), F32)
        gnbc = sb("gnbc", (128, DC), F32)
        workA = sb("workA", (128, 2048), F32)
        workB = sb("workB", (128, 2048), F32)
        workC = sb("workC", (128, 2048), F32)
        workE = sb("workE", (128, 1024), F32)
        tmA = sb("tmA", (128, 2048), BF16)
        qgtm = tmA
        tmB = sb("tmB", (128, 2048), BF16)
        kxtm = tmB
        kgtm = sb("kgtm", (128, TT * 1024), BF16)
        vtm = sb("vtm", (128, TT * 1024), BF16)
        qgf = sb("qgf", (128, H * C), BF16)
        kxf = sb("kxf", (128, H * C), BF16)
        silg = sb("silg", (128, DC * C), BF16)
        stm = tmA
        yev = sb("yev", (128, 1024), F32)
        bfB = sb("bfB", (128, 2048), BF16)
        xn1f = bfB
        gat = bfB
        bfA = sb("bfA", (128, 2048), BF16)
        yhf = bfA
        a1f = bfA
        r2q = tmB
        halo_t = sb("halo", (128, 3 * JC), BF16)
        def halo_ap(s):
            return halo_t[:, s * JC:(s + 1) * JC]
        s2bc = sb("s2bc", (128, C), F32)
        is2bc = sb("is2bc", (128, C), F32)
        cols = sb("cols", (128, 72), F32)
        mxq = workE[:, 0:256]
        rowsb = workE[0:1, 512:1024]
        gcols = sb("gcols", (128, 80), F32)
        pA = psal("pA", (128, 2048), F32)
        pB = psal("pB", (128, 2048), F32)
        sems = {e: stack.enter_context(nc.semaphore("sem_" + e)) for e in SEMS}

        S = Sched()
        COLS = {}
        PENDING_PROJV = []
        PENDING_YIN = []
        PENDING_STATE = []

        def col(name):
            if name not in COLS:
                COLS[name] = len(COLS)
                assert len(COLS) <= 72, f"cols overflow: {name}"
            i = COLS[name]
            return cols[:, i:i + 1]

        dlane = [0]
        lane_busy = {}

        def dma(dst_ap, src_ap, reads=(), writes=(), eng="sync"):
            if eng == "gpsimd":
                lane = "gdma"
            else:
                lane = f"dsync{dlane[0] % NDL}"
                dlane[0] += 1
            prev = lane_busy.get(lane)
            rd = tuple(reads) + ((prev,) if prev else ())
            tok = S.op(eng, lambda E_, s=src_ap, d=dst_ap: E_.dma_start(d, s),
                       reads=rd, writes=writes, inc=16, sem=lane)
            lane_busy[lane] = "lane_" + lane
            S.last_w["lane_" + lane] = tok
            return tok

        def dump(label, ap, deps):
            if label in dbg_ap:
                dma(dbg_ap[label], ap, reads=deps)

        pTb = pA[:].bitcast(BF16)[:, 0:2048]
        pTf = pA[:, 0:1024]
        pR = pB[:, 0:512]

        for d_, s_, k_ in ((gi_d, gic, "gic"), (gm_d, gmc, "gmc"), (gd_d, gdc, "gdc"),
                           (tri0_d, tri0, "tri0"),
                           (sv_d, svc, "svc"), (uv_d, uvc, "uvc"),
                           (id_d, ident, "ident"), (idf_d, identf, "identf"),
                           (onr_d, onesr, "onesr"), (onc_d, onesc, "onesc"), (kc_d, kconsts, "kconsts"),
                           (convw_d, cwc, "cwc"), (convb_d, cbc, "cbc"),
                           (gnw_d, gnwc, "gnwc"), (gnb_d, gnbc, "gnbc")):
            dma(s_[:], d_, writes=[k_])
        dma(wq[:], wq_p, writes=["wq"], eng="gpsimd")
        dma(wk[:], wk_p, writes=["wk"], eng="gpsimd")
        dma(xs_ap(0).rearrange("p (tt d) -> p tt d", tt=TT), x_in[0:C, :].rearrange("(tt t) d -> t tt d", t=128),
            writes=["x0", "xsl0"])
        for i in range(4):
            dma(tabs_ap(0, i), tab_d[i][0], writes=[f"tab{i}"])
        S.op("vector", lambda E_: E_.memset(st2f[:], 0.0), writes=["st2f"])
        S.op("vector", lambda E_: E_.memset(st2b[:], 0.0), writes=["st2b"])
        S.op("vector", lambda E_: E_.memset(halo_ap(2), 0.0), writes=["halo2"])
        for wsrc, wdst, key in ((wv_p, wv_s, "wv_s"), (wg_p, wg_s, "wg_s"), (wo_p, wo_s, "wo_s")):
            dma(strA[:], wsrc, writes=["strA"], eng="gpsimd")
            dma(wdst, strA[:], reads=["strA"], writes=[key])

        def quant_weights(w_src, r_dst, wtag):
            acc = gcols[:, 0:16]
            strAf = strA[:].bitcast(F32)[:, 0:2048]
            for i in range(16):
                pb, pk = (workA[:].bitcast(F32), "workA") if i % 2 == 0 else (strAf, "strA")
                dma(pb, w_src[:, i * 2048:(i + 1) * 2048], writes=[pk])
                S.op("scalar", lambda E_, pb=pb: E_.activation(workB[:], pb, AF.Abs),
                     reads=[pk], writes=["workB"])
                S.op("vector", lambda E_, i=i: E_.reduce_sum(acc[:, i:i + 1], workB[:], AX.X),
                     reads=["workB"], writes=[f"{wtag}a{i}"])
            S.op("vector", lambda E_: E_.reduce_sum(col(wtag + "ps"), acc[:], AX.X),
                 reads=[f"{wtag}a{i}" for i in range(16)], writes=[wtag + "ps"])
            S.op("tensor", lambda E_: E_.matmul(pR[0:1, 0:1], onesc[:], col(wtag + "ps"),
                                                start=True, stop=True),
                 reads=[wtag + "ps", "onesc"], writes=["pR"])
            S.op("vector", lambda E_: E_.tensor_scalar(rowsb[0:1, 0:1], pR[0:1, 0:1],
                                                       1.0 / (DH * D), 1e-5, ALU.mult, ALU.max),
                 reads=["pR"], writes=["rowsb", "workE"])
            S.op("tensor", lambda E_: E_.matmul(pR[:, 4:5], onesr[:], rowsb[0:1, 0:1],
                                                start=True, stop=True),
                 reads=["rowsb", "workE", "onesr"], writes=["pR"])
            S.op("vector", lambda E_, w=wtag: E_.tensor_scalar(col(w + "isw"), pR[:, 4:5],
                                                               1.0, None, ALU.mult),
                 reads=["pR"], writes=[wtag + "isw"])
            S.op("vector", lambda E_, w=wtag: E_.tensor_scalar(col(w + "thr"), pR[:, 4:5],
                                                               0.5, None, ALU.mult),
                 reads=["pR"], writes=[wtag + "thr"])
            S.op("vector", lambda E_, w=wtag: E_.tensor_scalar(col(w + "nth"), col(w + "thr"),
                                                               -1.0, None, ALU.mult),
                 reads=[wtag + "thr"], writes=[wtag + "nth"])
            for i in range(16):
                pb, pk = (workA[:].bitcast(F32), "workA") if i % 2 == 0 else (strAf, "strA")
                dma(pb, w_src[:, i * 2048:(i + 1) * 2048], writes=[pk])
                S.op("vector", lambda E_, w=wtag, pb=pb: E_.tensor_scalar(
                    workB[:], pb, col(w + "thr"), None, ALU.is_ge),
                    reads=[pk, wtag + "thr"], writes=["workB"])
                S.op("vector", lambda E_, w=wtag, pb=pb: E_.tensor_scalar(
                    workC[:], pb, col(w + "nth"), None, ALU.is_le),
                    reads=[pk, wtag + "nth"], writes=["workC"])
                rq = r2q[:].bitcast(FP8)[:, 0:2048]
                S.op("vector", lambda E_, rq=rq: E_.tensor_tensor(rq, workB[:], workC[:], ALU.subtract),
                     reads=["workB", "workC"], writes=["r2q", "tmB"])
                if wtag == "w1":
                    half, off = i % 2, (i // 2) * 2048
                else:
                    half, off = i // 8, (i % 8) * 2048
                dma(r_dst[half][:, off:off + 2048], rq, reads=["r2q"], writes=[f"{wtag}_s"])

        quant_weights(w1_p, r1_s, "w1")
        quant_weights(w2_p, r2_s, "w2")
        dma(strA[:], wv_s, reads=["wv_s"], writes=["strA"])
        dma(strB[:], wg_s, reads=["wg_s"], writes=["strB"])

        def ln_stats(tag, src_ap_fn, readkeys, ss="A", readkeys2=None):
            for tt in range(TT):
                a = src_ap_fn(tt)
                if tt == 1 and readkeys2 is not None:
                    readkeys = readkeys2
                S.op("vector", lambda E_, a=a, t=tag, tt=tt: E_.reduce_sum(
                    col("ln%ss%d" % (ss, tt)), a, AX.X), reads=readkeys, writes=["ln%ss%d" % (ss, tt)])
                S.op("scalar", lambda E_, a=a, t=tag, tt=tt: E_.activation(
                    silg[:, 0:1024], a, AF.Square, accum_out=col("ln%sq%d" % (ss, tt))),
                    reads=readkeys, writes=["ln%sq%d" % (ss, tt), "silg"])
                S.op("vector", lambda E_, t=tag, tt=tt: E_.tensor_scalar(
                    col("ln%sm%d" % (ss, tt)), col("ln%ss%d" % (ss, tt)), 1.0 / 1024, None, ALU.mult),
                    reads=["ln%ss%d" % (ss, tt)], writes=["ln%sm%d" % (ss, tt)])
                S.op("vector", lambda E_, t=tag, tt=tt: E_.tensor_scalar(
                    col("ln%sm2%d" % (ss, tt)), col("ln%sm%d" % (ss, tt)), col("ln%sm%d" % (ss, tt)), None, ALU.mult),
                    reads=["ln%sm%d" % (ss, tt)], writes=["ln%sm2%d" % (ss, tt)])
                S.op("vector", lambda E_, t=tag, tt=tt: E_.scalar_tensor_tensor(
                    col("ln%sv%d" % (ss, tt)), col("ln%sq%d" % (ss, tt)), 1.0 / 1024, col("ln%sm2%d" % (ss, tt)),
                    ALU.mult, ALU.subtract),
                    reads=["ln%sq%d" % (ss, tt), "ln%sm2%d" % (ss, tt)], writes=["ln%sv%d" % (ss, tt)])
                S.op("scalar", lambda E_, t=tag, tt=tt: E_.activation(
                    col("ln%sd%d" % (ss, tt)), col("ln%sv%d" % (ss, tt)), AF.Sqrt, bias=EPS),
                    reads=["ln%sv%d" % (ss, tt)], writes=["ln%sd%d" % (ss, tt)])
                S.op("vector", lambda E_, t=tag, tt=tt: E_.reciprocal(
                    col("ln%sr%d" % (ss, tt)), col("ln%sd%d" % (ss, tt))),
                    reads=["ln%sd%d" % (ss, tt)], writes=["ln%sr%d" % (ss, tt)])

        pTbB = pB[:].bitcast(BF16)[:, 0:2048]

        def transposes(pairs, inkeys, use_f32=False, bank="A", mid=None):
            idn = identf if use_f32 else ident
            if bank == "A":
                tgt, wkey = (pTf if use_f32 else pTb), "pT"
            else:
                tgt, wkey = pTbB, "pB"
            mk = "pTm" if bank == "A" else "pBm"
            n = len(pairs)
            for i, (src, off) in enumerate(pairs):
                marker = [mk] if (mid is not None and i == 7) else []
                S.op("tensor", lambda E_, s=src, o=off, tgt=tgt, idn=idn: E_.transpose(
                    tgt[:, o:o + 128], s, idn[:]),
                    reads=tuple(inkeys) + ("ident", "identf"),
                    writes=[wkey] + marker, inc=1)
                if mid is not None and i == 7:
                    mid(mk)

        def attn_front(c, part=0):
            xsl = xs_ap(c % 3)
            slk = f"xsl{c % 3}"
            xk = f"x{c}"
            tb = c % 2
            if part == 2:
                return attn_scores(c)
            if c > 0:
                for i in range(4):
                    dma(tabs_ap(0, i), tab_d[i][c], writes=[f"tab{i}"])
            if reached("ln1"):
                return
            ss = "A"
            ln_stats("l1", lambda tt: xsl[:, tt * 1024:(tt + 1) * 1024], [xk, slk], ss="A")
            xn1tm = tmA[:]
            for tt in range(TT):
                S.op("vector", lambda E_, tt=tt: E_.tensor_scalar(
                    xn1tm[:, tt * 1024:(tt + 1) * 1024], xsl[:, tt * 1024:(tt + 1) * 1024],
                    col(f"ln{ss}m{tt}"), col(f"ln{ss}r{tt}"), ALU.subtract, ALU.mult),
                    reads=[xk, slk, f"ln{ss}m{tt}", f"ln{ss}r{tt}"], writes=["xn1tm", "tmA"])
            dump(f"xsl_{c}", xsl, [xk, slk])
            dump(f"xn1tm_{c}", xn1tm[:, 0:2048], ["xn1tm"])
            transposes([(xn1tm[:, tt * 1024 + dc * 128: tt * 1024 + dc * 128 + 128],
                         dc * C + tt * 128) for tt in range(TT) for dc in range(DC)],
                       ["xn1tm", "tmA"])
            S.op("vector", lambda E_: E_.tensor_copy(xn1f[:], pTb[:, 0:DC * C]),
                 reads=["pT"], writes=["xn1f", "bfB"])
            dump(f"xn1f_{c}", xn1f[:], ["xn1f"])
            if reached("proj"):
                return

            for tt in range(TT):
                if False:
                    pass
                def projqk(E_, tt=tt):
                    last = None
                    for dc in range(DC):
                        lt = xn1f[:, dc * C + tt * 128: dc * C + tt * 128 + 128]
                        for wsb, off in ((wq, 0), (wk, 1024)):
                            for hf in range(2):
                                last = E_.matmul(
                                    pA[:, off + hf * 512: off + hf * 512 + 512], lt,
                                    wsb[:, dc * 1024 + hf * 512: dc * 1024 + hf * 512 + 512],
                                    start=(dc == 0), stop=(dc == DC - 1))
                    return last
                S.op("tensor", projqk, reads=["xn1f", "wq", "wk"], writes=["pA"])

                def projv(E_, tt=tt):
                    last = None
                    for dc in range(DC):
                        lt = xn1f[:, dc * C + tt * 128: dc * C + tt * 128 + 128]
                        for hf in range(2):
                            last = E_.matmul(
                                pB[:, hf * 512: hf * 512 + 512], lt,
                                strA[:, dc * 1024 + hf * 512: dc * 1024 + hf * 512 + 512],
                                start=(dc == 0), stop=(dc == DC - 1))
                    return last
                PENDING_PROJV.append((c, tt, projv))
                if "xpos" in SKIP:
                    continue
                for name, poff, stab, ctab in (("q", 0, 1, 0), ("k", 1024, 3, 2)):
                    st_ap = tabs_ap(0, stab)[:, tt * 128:(tt + 1) * 128]
                    ct_ap = tabs_ap(0, ctab)[:, tt * 128:(tt + 1) * 128]

                    wb4 = workB[:, 0:1024].rearrange("p (h s e) -> p h s e", h=H, s=2)
                    st4lo = st_ap[:, 0:64].unsqueeze(1).unsqueeze(2).broadcast_to((128, H, 1, 64))
                    st4hi = st_ap[:, 64:128].unsqueeze(1).unsqueeze(2).broadcast_to((128, H, 1, 64))
                    pA4 = pA[:, poff:poff + 1024].rearrange("p (h s e) -> p h s e", h=H, s=2)

                    def swmul(E_, wb4=wb4, st4lo=st4lo, st4hi=st4hi, pA4=pA4):
                        E_.tensor_tensor(wb4[:, :, 0:1, :], pA4[:, :, 1:2, :], st4lo, ALU.mult)
                        return E_.tensor_tensor(wb4[:, :, 1:2, :], pA4[:, :, 0:1, :], st4hi, ALU.mult)
                    S.op("vector", swmul, reads=["pA", f"tab{stab}"], writes=["workB"])
                    ct3 = ct_ap.unsqueeze(1).broadcast_to((128, H, 128))
                    wc3 = workC[:, 0:1024].rearrange("p (h e) -> p h e", h=H)
                    pA3 = pA[:, poff:poff + 1024].rearrange("p (h e) -> p h e", h=H)

                    def cmul(E_, wc3=wc3, ct3=ct3, pA3=pA3):
                        return E_.tensor_tensor(wc3, pA3, ct3, ALU.mult)
                    S.op("vector", cmul, reads=["pA", f"tab{ctab}"], writes=["workC"])
                    if name == "q":
                        S.op("vector", lambda E_: E_.tensor_tensor(
                            workB[:, 0:1024], workB[:, 0:1024], workC[:, 0:1024], ALU.add),
                            reads=["workB", "workC"], writes=["workB"])

                        def qg_fn(E_, tt=tt):
                            last = None
                            for h in range(H):
                                last = E_.tensor_scalar(
                                    qgtm[:, tt * 1024 + h * 128: tt * 1024 + (h + 1) * 128],
                                    workB[:, h * 128:(h + 1) * 128],
                                    gic[:, tt * H + h: tt * H + h + 1], None, ALU.mult)
                            return last
                        S.op("vector", qg_fn2 if False else qg_fn, reads=["workB", "gic"], writes=[f"qgtm{tt}", "tmA"])
                    else:
                        S.op("vector", lambda E_, tt=tt: E_.tensor_tensor(
                            kxtm[:, tt * 1024:(tt + 1) * 1024], workB[:, 0:1024],
                            workC[:, 0:1024], ALU.add),
                            reads=["workB", "workC"], writes=[f"kxtm{tt}", "tmB"])

                        def kg_fn(E_, tt=tt):
                            last = None
                            for h in range(H):
                                last = E_.tensor_scalar(
                                    kgtm[:, tt * 1024 + h * 128: tt * 1024 + (h + 1) * 128],
                                    kxtm[:, tt * 1024 + h * 128: tt * 1024 + (h + 1) * 128],
                                    gdc[:, tt * H + h: tt * H + h + 1], None, ALU.mult)
                            return last
                        S.op("vector", kg_fn, reads=[f"kxtm{tt}", "gdc"], writes=[f"kgtm{tt}"])
            if reached("trans"):
                return
            transposes([(qgtm[:, tt * 1024 + h * 128: tt * 1024 + h * 128 + 128],
                         h * C + tt * 128) for h in range(H) for tt in range(TT)],
                       ["qgtm0", "qgtm1"],
                       mid=lambda mk: S.op("vector", lambda E_: E_.tensor_copy(
                           qgf[:, 0:1024], pTb[:, 0:1024]), reads=[mk], writes=["qgh0"]))
            S.op("vector", lambda E_: E_.tensor_copy(qgf[:, 1024:H * C], pTb[:, 1024:H * C]),
                 reads=["pT"], writes=["qgf"])
            transposes([(kxtm[:, tt * 1024 + h * 128: tt * 1024 + h * 128 + 128],
                         h * C + tt * 128) for h in range(H) for tt in range(TT)],
                       ["kxtm0", "kxtm1"],
                       mid=lambda mk: S.op("vector", lambda E_: E_.tensor_copy(
                           kxf[:, 0:1024], pTb[:, 0:1024]), reads=[mk], writes=["kxh0"]))
            S.op("vector", lambda E_: E_.tensor_copy(kxf[:, 1024:H * C], pTb[:, 1024:H * C]),
                 reads=["pT"], writes=["kxf"])
            dump(f"qgf_{c}", qgf[:], ["qgf"])
            dump(f"kxf_{c}", kxf[:], ["kxf"])
            dump(f"vtm_{c}", vtm[:], ["vtm0", "vtm1"])
            if part == 1:
                return
            attn_scores(c)

        def attn_scores(c):
            if reached("g"):
                return
            if reached("scores"):
                return
            for mt in range(TT):
                def sc_fn(E_, mt=mt, hr=range(H)):
                    last = None
                    for h in hr:
                        last = E_.matmul(pA[:, h * 256:(h + 1) * 256],
                                         kxf[:, h * C + mt * 128: h * C + mt * 128 + 128],
                                         qgf[:, h * C:(h + 1) * C],
                                         start=(h % 2 == 0), stop=(h % 2 == 1))
                    return last

                def mask_fn(E_, mt=mt, hr=range(H)):
                    last = None
                    for h in hr:
                        if mt == 0:
                            last = E_.scalar_tensor_tensor(
                                stm[:, h * 256:(h + 1) * 256], pA[:, h * 256:(h + 1) * 256],
                                gmc[:, h:h + 1], tri0[:], ALU.mult, ALU.mult)
                        else:
                            last = E_.scalar_tensor_tensor(
                                kxf[:, h * 128:(h + 1) * 128],
                                pA[:, h * 256 + 128:(h + 1) * 256],
                                gmc[:, H + h:H + h + 1], tri0[:, 0:128], ALU.mult, ALU.mult)
                    return last
                if mt == 0:
                    S.op("tensor", lambda E_: sc_fn(E_, 0, range(0, 4)),
                         reads=["qgh0", "kxh0"], writes=["pA", "pAh0"])
                    S.op("vector", lambda E_: mask_fn(E_, 0, range(0, 4)),
                         reads=["pAh0", "gmc", "tri0"], writes=["stm0", "tmA"])
                    S.op("tensor", lambda E_: sc_fn(E_, 0, range(4, H)),
                         reads=["qgf", "kxf"], writes=["pA"])
                    S.op("vector", lambda E_: mask_fn(E_, 0, range(4, H)),
                         reads=["pA", "gmc", "tri0"], writes=["stm0", "tmA"])
                else:
                    S.op("tensor", lambda E_: sc_fn(E_, 1, range(0, 4)),
                         reads=["qgh0", "kxh0"], writes=["pA", "pAh0"])
                    S.op("tensor", lambda E_: sc_fn(E_, 1, range(4, H)),
                         reads=["qgf", "kxf"], writes=["pA"])
                    S.op("vector", lambda E_: mask_fn(E_, 1, range(0, 4)),
                         reads=["pAh0", "gmc", "tri0"], writes=["sm1"])
                    S.op("vector", lambda E_: mask_fn(E_, 1, range(4, H)),
                         reads=["pA", "gmc", "tri0"], writes=["sm1"])

                PENDING_YIN.append((c, mt))

            if reached("y"):
                return

            return

        def attn_rest(c):
            xsl = xs_ap(c % 3)
            slk = f"xsl{c % 3}"
            xk = f"x{c}"
            if c + 1 < n_run:
                dma(xs_ap((c + 1) % 3).rearrange("p (tt d) -> p tt d", tt=TT),
                    x_in[(c + 1) * C:(c + 2) * C, :].rearrange("(tt t) d -> t tt d", t=128),
                    writes=[f"x{c+1}", f"xsl{(c + 1) % 3}"])
            for c2, tt, projv in [p for p in PENDING_PROJV if p[0] == c]:
                S.op("tensor", projv, reads=["xn1f", "strA"], writes=["pB"])
                S.op("vector", lambda E_, tt=tt: E_.tensor_copy(
                    vtm[:, tt * 1024:(tt + 1) * 1024], pB[:, 0:1024]),
                    reads=["pB"], writes=[f"vtm{tt}"])
            PENDING_PROJV[:] = [p for p in PENDING_PROJV if p[0] != c]
            dma(strA[:], wo_s, reads=["wo_s"], writes=["strA"])

            def gproj(E_):
                last = None
                for dc in range(DC):
                    for dco in range(DC):
                        last = E_.matmul(
                            pB[:, dco * 256:(dco + 1) * 256],
                            strB[:, dc * 1024 + dco * 128: dc * 1024 + dco * 128 + 128],
                            xn1f[:, dc * C:(dc + 1) * C],
                            start=(dc == 0 and dco % 2 == 0),
                            stop=(dc == DC - 1 and dco % 2 == 1))
                return last

            def emit_g():
                S.op("tensor", gproj, reads=["xn1f", "strB"], writes=["pB"])
                S.op("scalar", lambda E_: E_.activation(silg[:], pB[:], AF.Silu),
                     reads=["pB"], writes=["silg"])
                dma(strB[:].bitcast(FP8)[:, 0:DC * 2048], r1_s[0], reads=["w1_s"], writes=["strB"])
                dump(f"silg_{c}", silg[:], ["silg"])
            for c2, mt in [p for p in PENDING_YIN if p[0] == c]:
                def yin_fn(E_, mt=mt):
                    last = None
                    for h in range(H):
                        for it in (range(TT) if mt == 0 else range(1, TT)):
                            sap = (stm[:, h * 256 + it * 128: h * 256 + it * 128 + 128]
                                   if mt == 0 else kxf[:, h * 128:(h + 1) * 128])
                            last = E_.matmul(
                                pA[:, it * 1024 + h * 128: it * 1024 + (h + 1) * 128],
                                sap,
                                vtm[:, mt * 1024 + h * 128: mt * 1024 + (h + 1) * 128],
                                start=(mt == 0 and h % 4 == 0), stop=False)
                    return last
                S.op("tensor", yin_fn,
                     reads=[("stm0" if mt == 0 else "sm1"), "vtm0", "vtm1"], writes=["pA"])
            PENDING_YIN[:] = [p for p in PENDING_YIN if p[0] != c]

            def yint_fn(E_):
                last = None
                for h in range(H):
                    for it in range(TT):
                        last = E_.matmul(
                            pA[:, it * 1024 + h * 128: it * 1024 + (h + 1) * 128],
                            qgf[:, h * C + it * 128: h * C + it * 128 + 128],
                            st2b[:, h * 128:(h + 1) * 128],
                            start=False, stop=(h % 4 == 3))
                return last
            S.op("tensor", yint_fn, reads=["qgf", "st2b"], writes=["pA"])
            emit_g()

            if reached("state"):
                return

            if reached("gn"):
                return
            yhtm = workA[:].bitcast(BF16)
            for tt in range(TT):
                ya = yev[:]
                S.op("vector", lambda E_, tt=tt: E_.tensor_copy(yev[:], pA[:, tt * 1024:(tt + 1) * 1024]),
                     reads=["pA"], writes=["yev"])
                dump(f"yraw_{c}_{tt}", yev[:], ["yev"])
                rk = ["yev"]
                y3 = ya.rearrange("p (h v) -> p h v", h=H)
                gsl = gcols[:, 16 + tt * 8: 24 + tt * 8]
                qsl = gcols[:, 32 + tt * 8: 40 + tt * 8]
                msl = gcols[:, 48 + tt * 8: 56 + tt * 8]
                vsl = gcols[:, 64 + tt * 8: 72 + tt * 8]
                S.op("vector", lambda E_, y3=y3, gsl=gsl: E_.reduce_sum(
                    gsl.unsqueeze(2), y3, AX.X), reads=rk, writes=[f"gns{tt}"])
                S.op("scalar", lambda E_, ya=ya: E_.activation(workE[:], ya, AF.Square),
                     reads=rk, writes=["workE"])
                S.op("vector", lambda E_, qsl=qsl: E_.reduce_sum(
                    qsl.unsqueeze(2), workE[:].rearrange("p (h v) -> p h v", h=H), AX.X),
                    reads=["workE"], writes=[f"gnq{tt}"])
                S.op("vector", lambda E_, gsl=gsl, msl=msl: E_.tensor_scalar(
                    msl, gsl, 1.0 / E, None, ALU.mult), reads=[f"gns{tt}"], writes=[f"gnm{tt}"])
                S.op("vector", lambda E_, msl=msl, vsl=vsl: E_.scalar_tensor_tensor(
                    vsl, msl, 0.0, msl, ALU.add, ALU.mult),
                    reads=[f"gnm{tt}"], writes=[f"gnm2{tt}"])
                S.op("vector", lambda E_, qsl=qsl, vsl=vsl: E_.scalar_tensor_tensor(
                    vsl, qsl, 1.0 / E, vsl, ALU.mult, ALU.subtract),
                    reads=[f"gnq{tt}", f"gnm2{tt}"], writes=[f"gnv{tt}"])
                S.op("scalar", lambda E_, vsl=vsl: E_.activation(vsl, vsl, AF.Sqrt, bias=EPS),
                     reads=[f"gnv{tt}"], writes=[f"gnd{tt}"])
                S.op("vector", lambda E_, vsl=vsl: E_.reciprocal(vsl, vsl),
                     reads=[f"gnd{tt}"], writes=[f"gnr{tt}"])

                def gn_app(E_, ya=ya, msl=msl, vsl=vsl):
                    last = None
                    for h in range(H):
                        last = E_.tensor_scalar(
                            ya[:, h * E:(h + 1) * E], ya[:, h * E:(h + 1) * E],
                            msl[:, h:h + 1], vsl[:, h:h + 1], ALU.subtract, ALU.mult)
                    return last
                S.op("vector", gn_app, reads=[f"gnm{tt}", f"gnr{tt}", "yev"],
                     writes=["yev"])
                S.op("vector", lambda E_, tt=tt: E_.tensor_copy(
                    yhtm[:, tt * 1024:(tt + 1) * 1024], yev[:]),
                    reads=["yev"], writes=["workA"])
            dump(f"yev_{c}", workA[:], ["workA"])

            def u_fn(E_):
                last = None
                for h in range(H):
                    for tt in range(TT):
                        last = E_.matmul(
                            pA[:, h * 128:(h + 1) * 128],
                            kgtm[:, tt * 1024 + h * 128: tt * 1024 + (h + 1) * 128],
                            vtm[:, tt * 1024 + h * 128: tt * 1024 + (h + 1) * 128],
                            start=(tt == 0 and h % 4 == 0), stop=(tt == TT - 1 and h % 4 == 3))
                return last
            S.op("tensor", u_fn, reads=["kgtm0", "kgtm1", "vtm0", "vtm1"], writes=["pA"])

            def supd_a(E_):
                last = None
                for h in range(H):
                    last = E_.tensor_scalar(
                        workE[:, h * 128:(h + 1) * 128], st2f[:, h * 128:(h + 1) * 128],
                        svc[:, h:h + 1], None, ALU.mult)
                return last

            def supd_b(E_):
                last = None
                for h in range(H):
                    last = E_.scalar_tensor_tensor(
                        st2f[:, h * 128:(h + 1) * 128], pA[:, h * 128:(h + 1) * 128],
                        uvc[:, h:h + 1], workE[:, h * 128:(h + 1) * 128], ALU.mult, ALU.add)
                return last
            S.op("vector", supd_a, reads=["st2f", "svc"], writes=["workE"])
            S.op("vector", supd_b, reads=["pA", "workE", "uvc"], writes=["st2f"])
            S.op("vector", lambda E_: E_.tensor_copy(st2b[:], st2f[:]),
                 reads=["st2f"], writes=["st2b"])
            if reached("gate"):
                return
            def gnwb_a(E_, hh=0):
                last = None
                for h in range(hh * 4, hh * 4 + 4):
                    last = E_.scalar_tensor_tensor(
                        workB[:, h * C:(h + 1) * C], yhf[:, h * C:(h + 1) * C], gnwc[:, h:h + 1],
                        gnbc[:, h:h + 1].broadcast_to((128, C)), ALU.mult, ALU.add)
                return last

            def gnwb_b(E_, hh=0):
                last = None
                for h in range(hh * 4, hh * 4 + 4):
                    last = E_.tensor_tensor(
                        gat[:, h * C:(h + 1) * C], workB[:, h * C:(h + 1) * C],
                        silg[:, h * C:(h + 1) * C], ALU.mult)
                return last

            def yh_mid(mk):
                S.op("vector", lambda E_: E_.tensor_copy(
                    yhf[:, 0:1024], pTb[:, 0:1024]), reads=[mk], writes=["yhfh0"])
                S.op("vector", lambda E_: gnwb_a(E_, 0), reads=["yhfh0", "gnwc", "gnbc"],
                     writes=["workB", "wBh0"])
                S.op("vector", lambda E_: gnwb_b(E_, 0), reads=["wBh0", "silg"],
                     writes=["gath0"])
            transposes([(yhtm[:, tt * 1024 + h * 128: tt * 1024 + h * 128 + 128],
                         h * C + tt * 128) for h in range(H) for tt in range(TT)],
                       ["workA"], mid=yh_mid)
            S.op("vector", lambda E_: E_.tensor_copy(yhf[:, 1024:H * C], pTb[:, 1024:H * C]),
                 reads=["pT"], writes=["yhf", "bfA"])
            S.op("vector", lambda E_: gnwb_a(E_, 1), reads=["yhf", "gnwc", "gnbc"],
                 writes=["workB"])
            S.op("vector", lambda E_: gnwb_b(E_, 1), reads=["workB", "silg"],
                 writes=["gat", "bfB"])
            dump(f"gat_{c}", gat[:], ["gat"])

            if reached("wo"):
                return

            def wo_fn(E_, tt=0):
                last = None
                for dc in range(DC):
                    lt = gat[:, dc * C + tt * 128: dc * C + tt * 128 + 128]
                    for hf in range(2):
                        last = E_.matmul(
                            pB[:, tt * 1024 + hf * 512: tt * 1024 + hf * 512 + 512], lt,
                            strA[:, dc * 1024 + hf * 512: dc * 1024 + hf * 512 + 512],
                            start=(dc == 0), stop=(dc == DC - 1))
                return last
            S.op("tensor", lambda E_: wo_fn(E_, 0), reads=["gat", "strA"],
                 writes=["pB", "pBt0"])
            S.op("vector", lambda E_: E_.tensor_copy(workB[:, 0:1024], pB[:, 0:1024]),
                 reads=["pBt0"], writes=["workB", "wBt0"])
            S.op("tensor", lambda E_: wo_fn(E_, 1), reads=["gat", "strA"], writes=["pB"])
            dma(strA[:].bitcast(FP8)[:, 0:DC * 2048], r1_s[1], reads=["w1_s"], writes=["strA"])
            if reached("ln2"):
                return
            S.op("vector", lambda E_: E_.tensor_copy(workB[:, 1024:2048], pB[:, 1024:2048]),
                 reads=["pB"], writes=["workB"])
            ss = "A"
            ln_stats("l2", lambda tt: workB[:, tt * 1024:(tt + 1) * 1024], ["wBt0"], ss="A",
                     readkeys2=["workB"])
            for tt in range(TT):
                S.op("vector", lambda E_, tt=tt: E_.tensor_scalar(
                    workC[:, 0:1024], workB[:, tt * 1024:(tt + 1) * 1024],
                    col(f"ln{ss}m{tt}"), col(f"ln{ss}r{tt}"), ALU.subtract, ALU.mult),
                    reads=["workB", f"ln{ss}m{tt}", f"ln{ss}r{tt}"], writes=["workC"])
                S.op("vector", lambda E_, tt=tt: E_.tensor_tensor(
                    xsl[:, tt * 1024:(tt + 1) * 1024], xsl[:, tt * 1024:(tt + 1) * 1024],
                    workC[:, 0:1024], ALU.add),
                    reads=["workC", xk, slk], writes=[f"xp{c}" if tt == TT - 1 else f"xpp{c}", slk])
            dump(f"xp_{c}", xsl, [f"xp{c}", slk])

        def mm1_chunk(c):
            xsl = xs_ap(c % 3)
            slk = f"xsl{c % 3}"
            hb = hbf_ap(c % 2)
            hslk = f"hslot{c % 2}"
            hk = f"h{c}"
            if reached("a1"):
                return
            ss = "B"
            ln_stats("l3", lambda tt: xsl[:, tt * 1024:(tt + 1) * 1024], [f"xp{c}", slk], ss="B")
            a1tm = workC[:].bitcast(BF16)
            for tt in range(TT):
                S.op("vector", lambda E_, tt=tt: E_.tensor_reduce(
                    col("qhi"), xsl[:, tt * 1024:(tt + 1) * 1024], AX.X, ALU.max),
                    reads=[f"xp{c}", slk], writes=["qhi"])
                S.op("vector", lambda E_, tt=tt: E_.tensor_reduce(
                    col("qlo"), xsl[:, tt * 1024:(tt + 1) * 1024], AX.X, ALU.min),
                    reads=[f"xp{c}", slk], writes=["qlo"])
                S.op("vector", lambda E_, tt=tt: E_.tensor_tensor(
                    col("qd1"), col("qhi"), col(f"ln{ss}m{tt}"), ALU.subtract),
                    reads=["qhi", f"ln{ss}m{tt}"], writes=["qd1"])
                S.op("vector", lambda E_, tt=tt: E_.tensor_tensor(
                    col("qd2"), col(f"ln{ss}m{tt}"), col("qlo"), ALU.subtract),
                    reads=["qlo", f"ln{ss}m{tt}"], writes=["qd2"])
                S.op("vector", lambda E_: E_.tensor_tensor(
                    col("qd1"), col("qd1"), col("qd2"), ALU.max),
                    reads=["qd1", "qd2"], writes=["qd1"])
                S.op("vector", lambda E_, tt=tt: E_.tensor_tensor(
                    col("qd1"), col("qd1"), col(f"ln{ss}r{tt}"), ALU.mult),
                    reads=["qd1", f"ln{ss}r{tt}"], writes=["qd1"])
                S.op("vector", lambda E_, tt=tt: E_.tensor_scalar(
                    col(f"q1c{tt}"), col("qd1"), 1e-5, None, ALU.max),
                    reads=["qd1"], writes=[f"q1c{tt}"])
                S.op("vector", lambda E_, tt=tt: E_.reciprocal(
                    col(f"q1r{tt}"), col(f"q1c{tt}")),
                    reads=[f"q1c{tt}"], writes=[f"q1r{tt}"])
                S.op("vector", lambda E_, tt=tt: E_.tensor_scalar(
                    col("q1s"), col(f"q1r{tt}"), 127.0, None, ALU.mult),
                    reads=[f"q1r{tt}"], writes=["q1s"])
                S.op("vector", lambda E_, tt=tt: E_.tensor_tensor(
                    col("q1s"), col("q1s"), col(f"ln{ss}r{tt}"), ALU.mult),
                    reads=["q1s", f"ln{ss}r{tt}"], writes=["q1s"])
                S.op("vector", lambda E_, tt=tt: E_.tensor_scalar(
                    workB[:, 0:1024], xsl[:, tt * 1024:(tt + 1) * 1024],
                    col(f"ln{ss}m{tt}"), col("q1s"), ALU.subtract, ALU.mult),
                    reads=[f"xp{c}", slk, f"ln{ss}m{tt}", "q1s"], writes=["workB"])
                i32v = workE[:].bitcast(I32)
                S.op("vector", lambda E_, i32v=i32v: E_.tensor_scalar(
                    i32v, workB[:, 0:1024], 16384.0, None, ALU.add),
                    reads=["workB"], writes=["workE"])
                S.op("vector", lambda E_, i32v=i32v: E_.scalar_tensor_tensor(
                    workA[:, 0:1024], workB[:, 0:1024], 16383.5, i32v, ALU.add, ALU.is_ge),
                    reads=["workB", "workE"], writes=["workA"])
                S.op("vector", lambda E_, i32v=i32v: E_.tensor_tensor(
                    workB[:, 0:1024], workA[:, 0:1024], i32v, ALU.add),
                    reads=["workA", "workE"], writes=["workB"])
                S.op("vector", lambda E_, tt=tt: E_.tensor_scalar(
                    col(f"q1i{tt}"), col(f"q1c{tt}"), 1.0 / 127, col("w1isw"), ALU.mult, ALU.mult),
                    reads=[f"q1c{tt}", "w1isw"], writes=[f"q1i{tt}"])
                S.op("vector", lambda E_, tt=tt: E_.tensor_scalar(
                    a1tm[:, tt * 1024:(tt + 1) * 1024], workB[:, 0:1024],
                    -16384.0, col(f"q1i{tt}"), ALU.add, ALU.mult),
                    reads=["workB", f"q1i{tt}"], writes=["workC"])
            transposes([(a1tm[:, tt * 1024 + dc * 128: tt * 1024 + dc * 128 + 128],
                         dc * C + tt * 128) for tt in range(TT) for dc in range(DC)],
                       ["workC"])
            S.op("vector", lambda E_: E_.tensor_copy(a1f[:], pTb[:, 0:DC * C]),
                 reads=["pT"], writes=["a1f", "bfA"])
            dump(f"a1f_{c}", a1f[:], ["a1f"])
            if reached("mm1"):
                return
            for q in range(4):
                sbuf_w = strB if q < 2 else strA
                wkey = "strB" if q < 2 else "strA"
                ptgt, pkey = (pA, "pA") if q % 2 == 0 else (pB, "pB")

                def mm1_fn(E_, q=q, sbuf_w=sbuf_w, ptgt=ptgt):
                    last = None
                    w8 = sbuf_w[:].bitcast(FP8)
                    for j8 in range(8):
                        jloc = (q * 8 + j8) % 16
                        for dc in range(DC):
                            last = E_.matmul(
                                ptgt[:, j8 * 256:(j8 + 1) * 256],
                                w8[:, dc * 2048 + jloc * 128: dc * 2048 + (jloc + 1) * 128],
                                a1f[:, dc * C:(dc + 1) * C],
                                start=(dc == 0 and j8 % 2 == 0),
                                stop=(dc == DC - 1 and j8 % 2 == 1))
                    return last
                S.op("tensor", mm1_fn, reads=["a1f", wkey], writes=[pkey])
                hdst = hb.rearrange("p (j s) -> p j s", s=HS)[:, q * 8:(q + 1) * 8, 1:257]
                S.op("scalar", lambda E_, ptgt=ptgt, hdst=hdst: E_.copy(
                    hdst, ptgt[:].rearrange("p (j t) -> p j t", t=256)),
                    reads=[pkey], writes=[f"{hk}q{q}", hslk])
                if c >= 1:
                    pb3 = hbf_ap((c - 1) % 2).rearrange("p (j s) -> p j s", s=HS)
                    S.op("vector", lambda E_, q=q, pb3=pb3: E_.tensor_copy(
                        pb3[:, q * 8:(q + 1) * 8, 257:258],
                        hb.rearrange("p (j s) -> p j s", s=HS)[:, q * 8:(q + 1) * 8, 1:2]),
                        reads=[f"{hk}q{q}", hslk, f"hslot{(c - 1) % 2}"],
                        writes=[f"h{c-1}hr{q}"])
                if q == 1:
                    dma(strB[:].bitcast(FP8)[:, 0:16 * D], r2_s[0], reads=["w2_s"], writes=["strB"])
                if q == 3:
                    dma(strA[:].bitcast(FP8)[:, 0:16 * D], r2_s[1], reads=["w2_s"], writes=["strA"])
            hsrc = hb.rearrange("p (j s) -> p j s", s=HS)[:, :, 256:257]
            S.op("vector", lambda E_, hsrc=hsrc: E_.tensor_copy(
                halo_ap(c % 3).unsqueeze(2), hsrc),
                reads=[f"{hk}q{q}" for q in range(4)] + [hslk], writes=[f"halo{c % 3}"])
            dump(f"h_{c}", hb, [f"{hk}q{q}" for q in range(4)] + [hslk])

        def ffn_conv(c):
            hb = hbf_ap(c % 2)
            hslk = f"hslot{c % 2}"
            hk = f"h{c}"
            h3 = hb.rearrange("p (j s) -> p j s", s=HS)
            xsl = xs_ap(c % 3)
            slk = f"xsl{c % 3}"
            if reached("conv"):
                return
            hrd = [f"{hk}q{q}" for q in range(4)]
            S.op("vector", lambda E_: E_.tensor_copy(
                h3[:, :, 0:1], halo_ap((c - 1) % 3).unsqueeze(2)),
                reads=[f"halo{(c - 1) % 3}"] + hrd + [hslk], writes=[hk + "hl", hslk])
            if c + 1 >= n_run:
                S.op("vector", lambda E_: E_.memset(h3[:, :, 257:258], 0.0),
                     writes=[f"{hk}hr{q}" for q in range(4)] + [hslk])
            hsk = []
            for q in range(4):
                j0 = q * 8
                def hq(sh, j0=j0):
                    return h3[:, j0:j0 + 8, sh:sh + 256]
                wB3 = workB[:].rearrange("p (j t) -> p j t", t=256)
                rks = [hk + "hl", f"{hk}hr{q}"] + hrd

                def tap0(E_, hq=hq, wB3=wB3, j0=j0):
                    last = None
                    for j8 in range(8):
                        jc = j0 + j8
                        last = E_.scalar_tensor_tensor(
                            wB3[:, j8], hq(0)[:, j8], cwc[:, jc * 3 + 0:jc * 3 + 1],
                            cbc[:, jc:jc + 1].broadcast_to((128, 256)), ALU.mult, ALU.add)
                    return last
                S.op("vector", tap0, reads=rks + [hslk, "cwc", "cbc"], writes=["workB"])

                def tap2(E_, hq=hq, wB3=wB3, j0=j0):
                    last = None
                    for j8 in range(8):
                        jc = j0 + j8
                        last = E_.scalar_tensor_tensor(
                            wB3[:, j8], hq(2)[:, j8], cwc[:, jc * 3 + 2:jc * 3 + 3],
                            wB3[:, j8], ALU.mult, ALU.add)
                    return last
                S.op("vector", tap2, reads=rks + [hslk, "cwc", "workB"], writes=["workB"])

                def cen_fn(E_, hq=hq, wB3=wB3, j0=j0):
                    last = None
                    for j8 in range(8):
                        jc = j0 + j8
                        last = E_.scalar_tensor_tensor(
                            wB3[:, j8], hq(1)[:, j8], cwc[:, jc * 3 + 1:jc * 3 + 2],
                            wB3[:, j8], ALU.mult, ALU.add)
                    return last
                S.op("vector", cen_fn, reads=["workB", "cwc", hslk] + rks, writes=["workB"])
                S.op("scalar", lambda E_, q=q, h3=h3: E_.activation(
                    h3[:, q * 8:(q + 1) * 8, 1:257],
                    workB[:].rearrange("p (j t) -> p j t", t=256), AF.Silu),
                    reads=["workB"], writes=[f"hs{c}q{q}", hslk])
                hsk.append(f"hs{c}q{q}")
            dump(f"hs_{c}", hb, hsk + [hslk])

        def ffn_tail(c):
            if reached("q2"):
                return
            hb = hbf_ap(c % 2)
            hslk = f"hslot{c % 2}"
            hk = f"h{c}"
            h3 = hb.rearrange("p (j s) -> p j s", s=HS)
            xsl = xs_ap(c % 3)
            slk = f"xsl{c % 3}"
            hsk = [f"hs{c}q{q}" for q in range(4)]
            for q in range(4):
                S.op("vector", lambda E_, q=q, h3=h3: E_.tensor_reduce(
                    workE[:, q * 256:(q + 1) * 256].unsqueeze(2),
                    h3[:, q * 8:(q + 1) * 8, 1:257].transpose([0, 2, 1]), AX.X,
                    ALU.max, apply_absolute_value=True),
                    reads=[f"hs{c}q{q}", hslk], writes=[f"mxp{q}", "workE"])
            S.op("vector", lambda E_: E_.tensor_tensor(
                workE[:, 0:256], workE[:, 0:256], workE[:, 256:512], ALU.max),
                reads=["mxp0", "mxp1", "workE"], writes=["mxp0", "workE"])
            S.op("vector", lambda E_: E_.tensor_tensor(
                workE[:, 512:768], workE[:, 512:768], workE[:, 768:1024], ALU.max),
                reads=["mxp2", "mxp3", "workE"], writes=["mxp2", "workE"])
            S.op("vector", lambda E_: E_.tensor_tensor(
                workE[:, 0:256], workE[:, 0:256], workE[:, 512:768], ALU.max),
                reads=["mxp0", "mxp2", "workE"], writes=["mxq", "workE"])
            for i in range(2):
                S.op("tensor", lambda E_, i=i: E_.transpose(
                    pB[:, i * 128:(i + 1) * 128], workE[:, i * 128:(i + 1) * 128], identf[:]),
                    reads=["mxq", "workE", "identf"], writes=["pB"])
            for tt in range(TT):
                S.op("vector", lambda E_, tt=tt: E_.tensor_reduce(
                    col(f"q2x{tt}"), pB[:, tt * 128:(tt + 1) * 128], AX.X, ALU.max,
                    apply_absolute_value=True), reads=["pB"], writes=[f"q2x{tt}"])
                S.op("vector", lambda E_, tt=tt: E_.tensor_scalar(
                    col(f"q2c{tt}"), col(f"q2x{tt}"), 1e-5, None, ALU.max),
                    reads=[f"q2x{tt}"], writes=[f"q2c{tt}"])
                S.op("vector", lambda E_, tt=tt: E_.reciprocal(
                    col(f"q2r{tt}"), col(f"q2c{tt}")), reads=[f"q2c{tt}"], writes=[f"q2r{tt}"])
                S.op("vector", lambda E_, tt=tt: E_.tensor_scalar(
                    col(f"q2s{tt}"), col(f"q2r{tt}"), 127.0, None, ALU.mult),
                    reads=[f"q2r{tt}"], writes=[f"q2s{tt}"])
                S.op("vector", lambda E_, tt=tt: E_.tensor_scalar(
                    col(f"q2i{tt}"), col(f"q2c{tt}"), 1.0 / 127, col("w2isw"),
                    ALU.mult, ALU.mult), reads=[f"q2c{tt}", "w2isw"], writes=[f"q2i{tt}"])
            for ci, nm in enumerate(("q2s0", "q2s1", "q2i0", "q2i1")):
                S.op("tensor", lambda E_, ci=ci, nm=nm: E_.transpose(
                    pB[0:1, 256 + ci * 128:256 + (ci + 1) * 128], col(nm), identf[:]),
                    reads=[nm, "identf", "pB"], writes=["pB"])
            for ci in range(4):
                S.op("vector", lambda E_, ci=ci: E_.tensor_copy(
                    rowsb[0:1, ci * 128:(ci + 1) * 128],
                    pB[0:1, 256 + ci * 128:256 + (ci + 1) * 128]),
                    reads=["pB"], writes=[f"row{ci}", "workE"])
            S.op("tensor", lambda E_: E_.matmul(pB[:, 1024:1280], onesr[:], rowsb[0:1, 0:256],
                                                start=True, stop=True),
                 reads=["row0", "row1", "workE", "onesr"], writes=["pB"])
            S.op("tensor", lambda E_: E_.matmul(pB[:, 1280:1536], onesr[:], rowsb[0:1, 256:512],
                                                start=True, stop=True),
                 reads=["row2", "row3", "workE", "onesr"], writes=["pB"])
            S.op("vector", lambda E_: E_.tensor_copy(s2bc[:], pB[:, 1024:1280]),
                 reads=["pB"], writes=["s2bc"])
            S.op("vector", lambda E_: E_.tensor_copy(is2bc[:], pB[:, 1280:1536]),
                 reads=["pB"], writes=["is2bc"])
            for q in range(4):
                hq2 = h3[:, q * 8:(q + 1) * 8, 1:257]
                s2b3 = s2bc[:].unsqueeze(1).broadcast_to((128, 8, 256))
                wA3 = workA[:].rearrange("p (j t) -> p j t", t=256)
                i32v = workC[:].bitcast(I32)
                S.op("vector", lambda E_, hq2=hq2, s2b3=s2b3, wA3=wA3: E_.tensor_tensor(
                    wA3, hq2, s2b3, ALU.mult), reads=hsk + [hslk, "s2bc"], writes=["workA"])
                S.op("vector", lambda E_, i32v=i32v: E_.tensor_scalar(
                    i32v, workA[:], 16384.0, None, ALU.add),
                    reads=["workA"], writes=["workC"])
                S.op("vector", lambda E_, i32v=i32v: E_.scalar_tensor_tensor(
                    workB[:], workA[:], 16383.5, i32v, ALU.add, ALU.is_ge),
                    reads=["workA", "workC"], writes=["workB"])
                S.op("vector", lambda E_, i32v=i32v: E_.scalar_tensor_tensor(
                    r2q[:], workB[:], -16384.0, i32v, ALU.add, ALU.add),
                    reads=["workB", "workC"], writes=["r2q", "tmB"])
                sbuf_w = strB if q < 2 else strA
                wkey = "strB" if q < 2 else "strA"

                def mm2_fn(E_, q=q, sbuf_w=sbuf_w):
                    last = None
                    w8 = sbuf_w[:].bitcast(FP8)
                    for j8 in range(8):
                        jloc = (q * 8 + j8) % 16
                        for dc in range(DC):
                            last = E_.matmul(
                                pB[:, dc * 256:(dc + 1) * 256],
                                w8[:, jloc * D + dc * 128: jloc * D + (dc + 1) * 128],
                                r2q[:, j8 * 256:(j8 + 1) * 256],
                                start=(q == 0 and j8 == 0 and dc % 2 == 0),
                                stop=(q == 3 and j8 == 7 and dc % 2 == 1))
                    return last
                S.op("tensor", mm2_fn, reads=["r2q", wkey], writes=["pB"])
            o2tm = workA[:].bitcast(BF16)
            is23 = is2bc[:].unsqueeze(1).broadcast_to((128, 4, 256))
            for hf in range(2):
                S.op("vector", lambda E_, hf=hf, is23=is23: E_.tensor_tensor(
                    yev[:].rearrange("p (d t) -> p d t", t=256),
                    pB[:, hf * 1024:(hf + 1) * 1024].rearrange("p (d t) -> p d t", t=256),
                    is23, ALU.mult),
                    reads=["pB", "is2bc"], writes=["yev"])
                S.op("vector", lambda E_, hf=hf: E_.tensor_copy(
                    o2tm[:, hf * 1024:(hf + 1) * 1024], yev[:]),
                    reads=["yev"], writes=["workA"])
            transposes([(o2tm[:, dc * 256 + tt * 128: dc * 256 + tt * 128 + 128],
                         tt * 1024 + dc * 128) for tt in range(TT) for dc in range(DC)],
                       ["workA"], bank="B",
                       mid=lambda mk: S.op("vector", lambda E_: E_.tensor_copy(
                           workB[:].bitcast(BF16)[:, 0:1024], pTbB[:, 0:1024]),
                           reads=[mk], writes=["o2h0"]))
            o2 = workB[:].bitcast(BF16)
            S.op("vector", lambda E_: E_.tensor_copy(o2[:, 1024:2048], pTbB[:, 1024:2048]),
                 reads=["pB"], writes=["workB"])
            dump(f"o2f_{c}", o2[:, 0:2048], ["workB"])
            ss = "C"
            ln_stats("l4", lambda tt: o2[:, tt * 1024:(tt + 1) * 1024], ["o2h0"], ss="C",
                     readkeys2=["workB"])
            dump(f"xpt_{c}", xsl, [f"xp{c}", slk])
            for tt in range(TT):
                S.op("vector", lambda E_, tt=tt: E_.tensor_scalar(
                    workC[:, 0:1024], o2[:, tt * 1024:(tt + 1) * 1024],
                    col(f"ln{ss}m{tt}"), col(f"ln{ss}r{tt}"), ALU.subtract, ALU.mult),
                    reads=["workB", f"ln{ss}m{tt}", f"ln{ss}r{tt}"], writes=["workC"])
                S.op("vector", lambda E_, tt=tt: E_.tensor_tensor(
                    xsl[:, tt * 1024:(tt + 1) * 1024], xsl[:, tt * 1024:(tt + 1) * 1024],
                    workC[:, 0:1024], ALU.add),
                    reads=["workC", f"xp{c}", slk], writes=[f"xf{c}t{tt}", slk])
                dma(out_d[c * C + tt * 128: c * C + (tt + 1) * 128, :],
                    xsl[:, tt * 1024:(tt + 1) * 1024],
                    reads=[f"xf{c}t{tt}", slk], writes=[f"store{c}t{tt}"])
                dump(f"xf_{c}_{tt}", xsl[:, tt * 1024:(tt + 1) * 1024],
                     [f"xf{c}t{tt}", slk])
            if c + 2 < n_run:
                dma(strA[:], wv_s, reads=["wv_s"], writes=["strA"])
                dma(strB[:], wg_s, reads=["wg_s"], writes=["strB"])

        attn_front(0)
        for c in range(n_run):
            attn_rest(c)
            mm1_chunk(c)
            if c >= 1:
                ffn_conv(c - 1)
            if c >= 1:
                ffn_tail(c - 1)
            if c + 1 < n_run:
                attn_front(c + 1, part=1)
                attn_front(c + 1, part=2)
            if c == 0 and n_run > 1:
                dma(strA[:], wv_s, reads=["wv_s"], writes=["strA"])
                dma(strB[:], wg_s, reads=["wg_s"], writes=["strB"])
        ffn_conv(n_run - 1)
        ffn_tail(n_run - 1)
        fcounts = [(s, S.count[s]) for s in SEMS if (s.startswith("dsync") or s == "gdma") and S.count[s]]

        def fence(E_, fcounts=fcounts):
            for s, v in fcounts:
                E_.wait_ge(sems[s], v)
            return None
        S.op("vector", fence, inc=0)

        S.emit_all(nc, sems)
    return nc


def kernel(**inputs):
    in_maps = host_prep(inputs)
    nc = build_program(NCH)
    res = run_bass_kernel_spmd(nc, in_maps, core_ids=list(range(NCORE)))
    out = np.stack([np.asarray(r["out"]) for r in res.results], axis=0)
    return out.astype(np.float32)


# version 32
# speedup vs baseline: 1.0162x; 1.0162x over previous
import math
import numpy as np
import ml_dtypes
import concourse.bass as bass
import concourse.mybir as mybir
from concourse.bass_utils import run_bass_kernel_spmd

F32 = mybir.dt.float32
BF16 = mybir.dt.bfloat16
FP8 = mybir.dt.float8e4
I32 = mybir.dt.int32
AF = mybir.ActivationFunctionType
ALU = mybir.AluOpType
AX = mybir.AxisListType

NCORE = 8
L, D = 4096, 1024
H, E = 8, 128
HALF = 64
DH = 4096
C = 256
NCH = L // C
TT = C // 128
JC = DH // 128
DC = D // 128
HS = 258
EPS = 1e-5
SCALE_BASE = 512.0

ENGS = ("tensor", "vector", "scalar", "gpsimd", "sync")


def _host_tables():
    t = {}
    perm = np.concatenate([np.arange(0, E, 2), np.arange(1, E, 2)])
    t["perm"] = perm
    pos = np.arange(L, dtype=np.float64)
    pairidx = np.arange(HALF, dtype=np.float64)
    base = (2 * pairidx + 0.4 * E) / (1.4 * E)
    inv_freq = 1.0 / 10000.0 ** (pairidx / HALF)
    sinus = pos[:, None] * inv_freq[None, :]
    sin, cos = np.sin(sinus), np.cos(sinus)
    i_rel = (np.arange(L) % C).astype(np.float64)
    scl = base[None, :] ** (i_rel[:, None] / SCALE_BASE)
    cq = np.concatenate([cos, cos], 1) * np.concatenate([scl, scl], 1)
    sq = np.concatenate([-sin, sin], 1) * np.concatenate([scl, scl], 1)
    ck = np.concatenate([cos, cos], 1) / np.concatenate([scl, scl], 1)
    sk = np.concatenate([-sin, sin], 1) / np.concatenate([scl, scl], 1)

    def tile_tab(a):
        a = a.reshape(NCH, TT, 128, E).transpose(0, 2, 1, 3).reshape(NCH, 128, TT * E)
        return np.ascontiguousarray(a.astype(np.float32))
    t["cq"], t["sq"], t["ck"], t["sk"] = map(tile_tab, (cq, sq, ck, sk))

    gam = 1.0 - np.exp(np.linspace(math.log(1 / 32), math.log(1 / 512), H))
    i = np.arange(C, dtype=np.float64)

    def colpack(a):
        return np.ascontiguousarray(
            a.reshape(TT, 128, H).transpose(1, 0, 2).reshape(128, TT * H).astype(np.float32))
    t["g_i"] = colpack(gam[None, :] ** i[:, None])
    t["g_negm"] = colpack(gam[None, :] ** (-i[:, None]))
    t["g_dec"] = colpack(gam[None, :] ** (C - 1.0 - i[:, None]))
    tri = (np.arange(C)[None, :] >= np.arange(C)[:, None]).astype(np.float32)
    t["tri0"] = np.ascontiguousarray(tri[0:128, :])
    t["tri1"] = np.ascontiguousarray(tri[128:256, 128:256])
    scC = np.concatenate([base, base]) ** (C / SCALE_BASE)
    t["svec"] = np.ascontiguousarray((scC[:, None] * gam[None, :] ** C).astype(np.float32))
    t["uvec"] = np.ascontiguousarray((scC[:, None] * gam[None, :] ** 1).astype(np.float32))
    t["ident"] = np.eye(128, dtype=ml_dtypes.bfloat16)
    t["identf"] = np.eye(128, dtype=np.float32)
    t["ones_row"] = np.ones((1, 128), dtype=np.float32)
    t["ones_col"] = np.ones((128, 1), dtype=np.float32)
    return t


def _pack_dxk(w, ncols):
    a = np.asarray(w, np.float32).reshape(DC, 128, ncols).transpose(1, 0, 2)
    return np.ascontiguousarray(a.reshape(128, DC * ncols))


def host_prep(inputs):
    t = _host_tables()
    perm = t["perm"]
    WQ = np.transpose(np.asarray(inputs["L0_attn_W_Q"], np.float32)[:, :, perm], (1, 0, 2)).reshape(D, H * E)
    WK = np.transpose(np.asarray(inputs["L0_attn_W_K"], np.float32)[:, :, perm], (1, 0, 2)).reshape(D, H * E)
    WV = np.transpose(np.asarray(inputs["L0_attn_W_V"], np.float32), (1, 0, 2)).reshape(D, H * E)
    shared = {
        "wq_p": _pack_dxk(WQ, 1024),
        "wk_p": _pack_dxk(WK, 1024),
        "wv_p": _pack_dxk(WV, 1024),
        "wg_p": _pack_dxk(np.asarray(inputs["L0_attn_W_G"], np.float32), 1024),
        "wo_p": _pack_dxk(np.asarray(inputs["L0_attn_W_O"], np.float32), 1024),
        "w1_p": _pack_dxk(np.asarray(inputs["L0_ffn_w1"], np.float32).T, DH),
        "w2_p": np.ascontiguousarray(
            np.asarray(inputs["L0_ffn_w2"], np.float32).T.reshape(JC, 128, D)
            .transpose(1, 0, 2).reshape(128, JC * D)),
        "convw": np.ascontiguousarray(
            np.asarray(inputs["L0_ffn_conv_w"], np.float32)[:, 0, :].reshape(JC, 128, 3)
            .transpose(1, 0, 2).reshape(128, JC * 3)),
        "convb": np.ascontiguousarray(
            np.asarray(inputs["L0_ffn_conv_b"], np.float32).reshape(JC, 128).T),
        "gnw_c": np.ascontiguousarray(
            np.asarray(inputs["L0_attn_gn_weight"], np.float32).reshape(DC, 128).T),
        "gnb_c": np.ascontiguousarray(
            np.asarray(inputs["L0_attn_gn_bias"], np.float32).reshape(DC, 128).T),
        "cq": t["cq"], "sq": t["sq"], "ck": t["ck"], "sk": t["sk"],
        "g_i": t["g_i"], "g_negm": t["g_negm"], "g_dec": t["g_dec"],
        "tri0": t["tri0"], "tri1": t["tri1"], "svec": t["svec"], "uvec": t["uvec"],
        "ident": t["ident"], "identf": t["identf"],
        "ones_row": t["ones_row"], "ones_col": t["ones_col"],
        "kconsts": np.broadcast_to(np.array([1e-5, 16384.0], np.float32), (128, 2)).copy(),
    }
    X = np.asarray(inputs["L0_attn_X"], np.float32)
    return [dict(shared, x_in=np.ascontiguousarray(X[b])) for b in range(NCORE)]


NDL = 6
SAME_ENGINE_SKIP = False
SEMS = ENGS + ("gdma",) + tuple(f"dsync{i}" for i in range(NDL))


class Sched:
    def __init__(self):
        self.ops = {e: [] for e in ENGS}
        self.count = {s: 0 for s in SEMS}
        self.know = {e: {s: 0 for s in SEMS} for e in ENGS}
        self.snaps = {s: [(0, {q: 0 for q in SEMS})] for s in SEMS}
        self.last_w = {}
        self.readers = {}

    ALIAS = {"pT": "pA", "pR": "pB"}
    BUFOF = (("xn1f", "bfB"), ("gat", "bfB"), ("yhf", "bfA"), ("a1f", "bfA"),
             ("r2q", "tmB"), ("kxtm", "tmB"), ("qgtm", "tmA"), ("stm", "tmA"),
             ("rowsb", "workE"), ("xn1tm", "tmA"), ("row0", "workE"), ("row1", "workE"),
             ("row2", "workE"), ("row3", "workE"), ("mxq", "workE"),
             ("sm1", "kxf"))

    def _expand(self, keys):
        out = []
        for k in keys:
            k = self.ALIAS.get(k, k)
            out.append(k)
            for pre, buf in self.BUFOF:
                if k.startswith(pre):
                    out.append(buf)
                    break
        return tuple(out)

    def op(self, eng, fn, reads=(), writes=(), rr=(), inc=1, sem=None):
        semn = sem or eng
        reads = self._expand(tuple(reads) + tuple(rr))
        writes = self._expand(writes)
        deps = []
        for k in reads + tuple(writes):
            tok = self.last_w.get(k)
            if tok is not None:
                deps.append(tok)
        for k in writes:
            deps.extend(self.readers.get(k, ()))
        waits = []
        kn = self.know[eng]
        for (pe, pv) in deps:
            if kn[pe] >= pv:
                continue
            if SAME_ENGINE_SKIP and pe == eng and not pe.startswith("d") and pe != "gdma":
                continue
            waits.append((pe, pv))
            kn[pe] = max(kn[pe], pv)
            snaps = self.snaps[pe]
            lo, hi, best = 0, len(snaps) - 1, None
            while lo <= hi:
                mid = (lo + hi) // 2
                if snaps[mid][0] <= pv:
                    best = snaps[mid][1]
                    lo = mid + 1
                else:
                    hi = mid - 1
            if best:
                for q, qv in best.items():
                    if qv > kn[q]:
                        kn[q] = qv
        self.count[semn] += inc
        tok = (semn, self.count[semn])
        if inc:
            self.snaps[semn].append((self.count[semn], dict(kn)))
        self.ops[eng].append((waits, fn, inc, semn))
        for k in reads:
            self.readers.setdefault(k, []).append(tok)
        for k in writes:
            self.last_w[k] = tok
            self.readers[k] = []
        return tok

    def emit_all(self, nc, sems):
        with nc.Block() as block:
            for eng in ENGS:
                ops = self.ops[eng]
                if not ops:
                    continue

                def body(engobj, ops=ops, eng=eng):
                    for (waits, fn, inc, semn) in ops:
                        for (pe, pv) in waits:
                            engobj.wait_ge(sems[pe], pv)
                        inst = fn(engobj)
                        if inc and inst is not None:
                            inst.then_inc(sems[semn], inc)
                getattr(block, eng)(body)


SKIP = set()
STAGES = ["p0", "p1", "ln1", "proj", "trans", "g", "scores", "y", "state",
          "gn", "gate", "wo", "ln2", "a1", "mm1", "conv", "q2", "mm2", "all"]


def build_program(n_run=NCH, dumps=(), stage="all"):
    stage_i = STAGES.index(stage)
    def reached(s):
        return STAGES.index(s) > stage_i
    nc = bass.Bass("TRN2", target_bir_lowering=False)
    dumps = {lbl: (shape, dt) for (lbl, shape, dt) in dumps}
    dbg_ap = {}

    def din(name, shape, dt):
        return nc.dram_tensor(name, list(shape), dt, kind="ExternalInput").ap()

    x_in = din("x_in", (L, D), F32)
    wq_p = din("wq_p", (128, DC * 1024), F32)
    wk_p = din("wk_p", (128, DC * 1024), F32)
    wv_p = din("wv_p", (128, DC * 1024), F32)
    wg_p = din("wg_p", (128, DC * 1024), F32)
    wo_p = din("wo_p", (128, DC * 1024), F32)
    w1_p = din("w1_p", (128, DC * DH), F32)
    w2_p = din("w2_p", (128, JC * D), F32)
    convw_d = din("convw", (128, JC * 3), F32)
    convb_d = din("convb", (128, JC), F32)
    gnw_d = din("gnw_c", (128, DC), F32)
    gnb_d = din("gnb_c", (128, DC), F32)
    tab_d = [din(n, (NCH, 128, TT * E), F32) for n in ("cq", "sq", "ck", "sk")]
    gi_d = din("g_i", (128, TT * H), F32)
    gm_d = din("g_negm", (128, TT * H), F32)
    gd_d = din("g_dec", (128, TT * H), F32)
    tri0_d = din("tri0", (128, 256), F32)
    tri1_d = din("tri1", (128, 128), F32)
    sv_d = din("svec", (128, H), F32)
    uv_d = din("uvec", (128, H), F32)
    id_d = din("ident", (128, 128), BF16)
    idf_d = din("identf", (128, 128), F32)
    kc_d = din("kconsts", (128, 2), F32)
    onr_d = din("ones_row", (1, 128), F32)
    onc_d = din("ones_col", (128, 1), F32)

    out_d = nc.dram_tensor("out", [L, D], F32, kind="ExternalOutput").ap()
    wv_s = nc.dram_tensor("wv_s", [128, DC * 1024], BF16, kind="Internal").ap()
    wg_s = nc.dram_tensor("wg_s", [128, DC * 1024], BF16, kind="Internal").ap()
    wo_s = nc.dram_tensor("wo_s", [128, DC * 1024], BF16, kind="Internal").ap()
    r1_s = nc.dram_tensor("r1_s", [2, 128, DC * 2048], FP8, kind="Internal").ap()
    r2_s = nc.dram_tensor("r2_s", [2, 128, 16 * D], FP8, kind="Internal").ap()
    for lbl, (shape, dt) in dumps.items():
        dbg_ap[lbl] = nc.dram_tensor("dbg_" + lbl, list(shape), dt, kind="ExternalOutput").ap()

    from contextlib import ExitStack
    with ExitStack() as stack:
        def sb(name, shape, dt):
            return stack.enter_context(nc.sbuf_tensor(name, list(shape), dt))

        def psal(name, shape, dt):
            return stack.enter_context(nc.psum_tensor(name, list(shape), dt))

        wq = sb("wq", (128, DC * 1024), BF16)
        wk = sb("wk", (128, DC * 1024), BF16)
        strA = sb("strA", (128, 8192), BF16)
        strB = sb("strB", (128, 8192), BF16)
        tabs_t = sb("tabs", (128, 4 * TT * E), F32)
        xs_t = sb("xs", (128, 3 * TT * 1024), F32)
        hbf_t = sb("hbf", (128, 2 * JC * HS), BF16)
        def tabs_ap(buf, i):
            return tabs_t[:, i * TT * E:(i + 1) * TT * E]
        def xs_ap(s):
            return xs_t[:, s * 2048:(s + 1) * 2048]
        def hbf_ap(b):
            return hbf_t[:, b * JC * HS:(b + 1) * JC * HS]
        st2f = sb("st2f", (128, H * E), F32)
        st2b = sb("st2b", (128, H * E), BF16)
        gic = sb("gic", (128, TT * H), F32)
        gmc = sb("gmc", (128, TT * H), F32)
        gdc = sb("gdc", (128, TT * H), F32)
        tri0 = sb("s_tri0", (128, 256), F32)
        svc = sb("svc", (128, H), F32)
        uvc = sb("uvc", (128, H), F32)
        ident = sb("s_id", (128, 128), BF16)
        identf = sb("s_idf", (128, 128), F32)
        kconsts = sb("kconsts_s", (128, 2), F32)
        nc.const_aps.register(F32, 1e-5, kconsts[:, 0:1])
        nc.const_aps.register(F32, 16384.0, kconsts[:, 1:2])
        onesr = sb("onesr", (1, 128), F32)
        onesc = sb("onesc", (128, 1), F32)
        cwc = sb("cwc", (128, JC * 3), F32)
        cbc = sb("cbc", (128, JC), F32)
        gnwc = sb("gnwc", (128, DC), F32)
        gnbc = sb("gnbc", (128, DC), F32)
        workA = sb("workA", (128, 2048), F32)
        workB = sb("workB", (128, 2048), F32)
        workC = sb("workC", (128, 2048), F32)
        workE = sb("workE", (128, 1024), F32)
        tmA = sb("tmA", (128, 2048), BF16)
        qgtm = tmA
        tmB = sb("tmB", (128, 2048), BF16)
        kxtm = tmB
        kgtm = sb("kgtm", (128, TT * 1024), BF16)
        vtm = sb("vtm", (128, TT * 1024), BF16)
        qgf = sb("qgf", (128, H * C), BF16)
        kxf = sb("kxf", (128, H * C), BF16)
        silg = sb("silg", (128, DC * C), BF16)
        stm = tmA
        yev = sb("yev", (128, 1024), F32)
        bfB = sb("bfB", (128, 2048), BF16)
        xn1f = bfB
        gat = bfB
        bfA = sb("bfA", (128, 2048), BF16)
        yhf = bfA
        a1f = bfA
        r2q = tmB
        halo_t = sb("halo", (128, 3 * JC), BF16)
        def halo_ap(s):
            return halo_t[:, s * JC:(s + 1) * JC]
        s2bc = sb("s2bc", (128, C), F32)
        is2bc = sb("is2bc", (128, C), F32)
        cols = sb("cols", (128, 72), F32)
        mxq = workE[:, 0:256]
        rowsb = workE[0:1, 512:1024]
        gcols = sb("gcols", (128, 80), F32)
        pA = psal("pA", (128, 2048), F32)
        pB = psal("pB", (128, 2048), F32)
        sems = {e: stack.enter_context(nc.semaphore("sem_" + e)) for e in SEMS}

        S = Sched()
        COLS = {}
        PENDING_PROJV = []
        PENDING_YIN = []
        PENDING_STATE = []

        def col(name):
            if name not in COLS:
                COLS[name] = len(COLS)
                assert len(COLS) <= 72, f"cols overflow: {name}"
            i = COLS[name]
            return cols[:, i:i + 1]

        dlane = [0]
        lane_busy = {}

        def dma(dst_ap, src_ap, reads=(), writes=(), eng="sync"):
            if eng == "gpsimd":
                lane = "gdma"
            else:
                lane = f"dsync{dlane[0] % NDL}"
                dlane[0] += 1
            prev = lane_busy.get(lane)
            rd = tuple(reads) + ((prev,) if prev else ())
            tok = S.op(eng, lambda E_, s=src_ap, d=dst_ap: E_.dma_start(d, s),
                       reads=rd, writes=writes, inc=16, sem=lane)
            lane_busy[lane] = "lane_" + lane
            S.last_w["lane_" + lane] = tok
            return tok

        def dump(label, ap, deps):
            if label in dbg_ap:
                dma(dbg_ap[label], ap, reads=deps)

        pTb = pA[:].bitcast(BF16)[:, 0:2048]
        pTf = pA[:, 0:1024]
        pR = pB[:, 0:512]

        for d_, s_, k_ in ((gi_d, gic, "gic"), (gm_d, gmc, "gmc"), (gd_d, gdc, "gdc"),
                           (tri0_d, tri0, "tri0"),
                           (sv_d, svc, "svc"), (uv_d, uvc, "uvc"),
                           (id_d, ident, "ident"), (idf_d, identf, "identf"),
                           (onr_d, onesr, "onesr"), (onc_d, onesc, "onesc"), (kc_d, kconsts, "kconsts"),
                           (convw_d, cwc, "cwc"), (convb_d, cbc, "cbc"),
                           (gnw_d, gnwc, "gnwc"), (gnb_d, gnbc, "gnbc")):
            dma(s_[:], d_, writes=[k_])
        dma(wq[:], wq_p, writes=["wq"], eng="gpsimd")
        dma(wk[:], wk_p, writes=["wk"], eng="gpsimd")
        dma(xs_ap(0).rearrange("p (tt d) -> p tt d", tt=TT), x_in[0:C, :].rearrange("(tt t) d -> t tt d", t=128),
            writes=["x0", "xsl0"])
        for i in range(4):
            dma(tabs_ap(0, i), tab_d[i][0], writes=[f"tab{i}"])
        S.op("vector", lambda E_: E_.memset(st2f[:], 0.0), writes=["st2f"])
        S.op("vector", lambda E_: E_.memset(st2b[:], 0.0), writes=["st2b"])
        S.op("vector", lambda E_: E_.memset(halo_ap(2), 0.0), writes=["halo2"])
        for wsrc, wdst, key in ((wv_p, wv_s, "wv_s"), (wg_p, wg_s, "wg_s"), (wo_p, wo_s, "wo_s")):
            dma(strA[:], wsrc, writes=["strA"], eng="gpsimd")
            dma(wdst, strA[:], reads=["strA"], writes=[key])

        def quant_weights(w_src, r_dst, wtag):
            acc = gcols[:, 0:16]
            strAf = strA[:].bitcast(F32)[:, 0:2048]
            for i in range(16):
                pb, pk = (workA[:].bitcast(F32), "workA") if i % 2 == 0 else (strAf, "strA")
                dma(pb, w_src[:, i * 2048:(i + 1) * 2048], writes=[pk])
                S.op("scalar", lambda E_, pb=pb: E_.activation(workB[:], pb, AF.Abs),
                     reads=[pk], writes=["workB"])
                S.op("vector", lambda E_, i=i: E_.reduce_sum(acc[:, i:i + 1], workB[:], AX.X),
                     reads=["workB"], writes=[f"{wtag}a{i}"])
            S.op("vector", lambda E_: E_.reduce_sum(col(wtag + "ps"), acc[:], AX.X),
                 reads=[f"{wtag}a{i}" for i in range(16)], writes=[wtag + "ps"])
            S.op("tensor", lambda E_: E_.matmul(pR[0:1, 0:1], onesc[:], col(wtag + "ps"),
                                                start=True, stop=True),
                 reads=[wtag + "ps", "onesc"], writes=["pR"])
            S.op("vector", lambda E_: E_.tensor_scalar(rowsb[0:1, 0:1], pR[0:1, 0:1],
                                                       1.0 / (DH * D), 1e-5, ALU.mult, ALU.max),
                 reads=["pR"], writes=["rowsb", "workE"])
            S.op("tensor", lambda E_: E_.matmul(pR[:, 4:5], onesr[:], rowsb[0:1, 0:1],
                                                start=True, stop=True),
                 reads=["rowsb", "workE", "onesr"], writes=["pR"])
            S.op("vector", lambda E_, w=wtag: E_.tensor_scalar(col(w + "isw"), pR[:, 4:5],
                                                               1.0, None, ALU.mult),
                 reads=["pR"], writes=[wtag + "isw"])
            S.op("vector", lambda E_, w=wtag: E_.tensor_scalar(col(w + "thr"), pR[:, 4:5],
                                                               0.5, None, ALU.mult),
                 reads=["pR"], writes=[wtag + "thr"])
            S.op("vector", lambda E_, w=wtag: E_.tensor_scalar(col(w + "nth"), col(w + "thr"),
                                                               -1.0, None, ALU.mult),
                 reads=[wtag + "thr"], writes=[wtag + "nth"])
            for i in range(16):
                pb, pk = (workA[:].bitcast(F32), "workA") if i % 2 == 0 else (strAf, "strA")
                dma(pb, w_src[:, i * 2048:(i + 1) * 2048], writes=[pk])
                S.op("vector", lambda E_, w=wtag, pb=pb: E_.tensor_scalar(
                    workB[:], pb, col(w + "thr"), None, ALU.is_ge),
                    reads=[pk, wtag + "thr"], writes=["workB"])
                S.op("vector", lambda E_, w=wtag, pb=pb: E_.tensor_scalar(
                    workC[:], pb, col(w + "nth"), None, ALU.is_le),
                    reads=[pk, wtag + "nth"], writes=["workC"])
                rq = r2q[:].bitcast(FP8)[:, 0:2048]
                S.op("vector", lambda E_, rq=rq: E_.tensor_tensor(rq, workB[:], workC[:], ALU.subtract),
                     reads=["workB", "workC"], writes=["r2q", "tmB"])
                if wtag == "w1":
                    half, off = i % 2, (i // 2) * 2048
                else:
                    half, off = i // 8, (i % 8) * 2048
                dma(r_dst[half][:, off:off + 2048], rq, reads=["r2q"], writes=[f"{wtag}_s"])

        quant_weights(w1_p, r1_s, "w1")
        quant_weights(w2_p, r2_s, "w2")
        dma(strA[:], wv_s, reads=["wv_s"], writes=["strA"])
        dma(strB[:], wg_s, reads=["wg_s"], writes=["strB"])

        def ln_stats(tag, src_ap_fn, readkeys, ss="A", readkeys2=None):
            for tt in range(TT):
                a = src_ap_fn(tt)
                if tt == 1 and readkeys2 is not None:
                    readkeys = readkeys2
                S.op("vector", lambda E_, a=a, t=tag, tt=tt: E_.reduce_sum(
                    col("ln%ss%d" % (ss, tt)), a, AX.X), reads=readkeys, writes=["ln%ss%d" % (ss, tt)])
                S.op("scalar", lambda E_, a=a, t=tag, tt=tt: E_.activation(
                    silg[:, 0:1024], a, AF.Square, accum_out=col("ln%sq%d" % (ss, tt))),
                    reads=readkeys, writes=["ln%sq%d" % (ss, tt), "silg"])
                S.op("vector", lambda E_, t=tag, tt=tt: E_.tensor_scalar(
                    col("ln%sm%d" % (ss, tt)), col("ln%ss%d" % (ss, tt)), 1.0 / 1024, None, ALU.mult),
                    reads=["ln%ss%d" % (ss, tt)], writes=["ln%sm%d" % (ss, tt)])
                S.op("vector", lambda E_, t=tag, tt=tt: E_.tensor_scalar(
                    col("ln%sm2%d" % (ss, tt)), col("ln%sm%d" % (ss, tt)), col("ln%sm%d" % (ss, tt)), None, ALU.mult),
                    reads=["ln%sm%d" % (ss, tt)], writes=["ln%sm2%d" % (ss, tt)])
                S.op("vector", lambda E_, t=tag, tt=tt: E_.scalar_tensor_tensor(
                    col("ln%sv%d" % (ss, tt)), col("ln%sq%d" % (ss, tt)), 1.0 / 1024, col("ln%sm2%d" % (ss, tt)),
                    ALU.mult, ALU.subtract),
                    reads=["ln%sq%d" % (ss, tt), "ln%sm2%d" % (ss, tt)], writes=["ln%sv%d" % (ss, tt)])
                S.op("scalar", lambda E_, t=tag, tt=tt: E_.activation(
                    col("ln%sd%d" % (ss, tt)), col("ln%sv%d" % (ss, tt)), AF.Sqrt, bias=EPS),
                    reads=["ln%sv%d" % (ss, tt)], writes=["ln%sd%d" % (ss, tt)])
                S.op("vector", lambda E_, t=tag, tt=tt: E_.reciprocal(
                    col("ln%sr%d" % (ss, tt)), col("ln%sd%d" % (ss, tt))),
                    reads=["ln%sd%d" % (ss, tt)], writes=["ln%sr%d" % (ss, tt)])

        pTbB = pB[:].bitcast(BF16)[:, 0:2048]

        def transposes(pairs, inkeys, use_f32=False, bank="A", mid=None):
            idn = identf if use_f32 else ident
            if bank == "A":
                tgt, wkey = (pTf if use_f32 else pTb), "pT"
            else:
                tgt, wkey = pTbB, "pB"
            mk = "pTm" if bank == "A" else "pBm"
            n = len(pairs)
            for i, (src, off) in enumerate(pairs):
                marker = [mk] if (mid is not None and i == 7) else []
                S.op("tensor", lambda E_, s=src, o=off, tgt=tgt, idn=idn: E_.transpose(
                    tgt[:, o:o + 128], s, idn[:]),
                    reads=tuple(inkeys) + ("ident", "identf"),
                    writes=[wkey] + marker, inc=1)
                if mid is not None and i == 7:
                    mid(mk)

        def attn_front(c, part=0):
            xsl = xs_ap(c % 3)
            slk = f"xsl{c % 3}"
            xk = f"x{c}"
            tb = c % 2
            if part == 2:
                return attn_scores(c)
            if c > 0:
                for i in range(4):
                    dma(tabs_ap(0, i), tab_d[i][c], writes=[f"tab{i}"])
            if reached("ln1"):
                return
            ss = "A"
            ln_stats("l1", lambda tt: xsl[:, tt * 1024:(tt + 1) * 1024], [xk, slk], ss="A")
            xn1tm = tmA[:]
            for tt in range(TT):
                S.op("vector", lambda E_, tt=tt: E_.tensor_scalar(
                    xn1tm[:, tt * 1024:(tt + 1) * 1024], xsl[:, tt * 1024:(tt + 1) * 1024],
                    col(f"ln{ss}m{tt}"), col(f"ln{ss}r{tt}"), ALU.subtract, ALU.mult),
                    reads=[xk, slk, f"ln{ss}m{tt}", f"ln{ss}r{tt}"], writes=["xn1tm", "tmA"])
            dump(f"xsl_{c}", xsl, [xk, slk])
            dump(f"xn1tm_{c}", xn1tm[:, 0:2048], ["xn1tm"])
            transposes([(xn1tm[:, tt * 1024 + dc * 128: tt * 1024 + dc * 128 + 128],
                         dc * C + tt * 128) for tt in range(TT) for dc in range(DC)],
                       ["xn1tm", "tmA"])
            S.op("vector", lambda E_: E_.tensor_copy(xn1f[:], pTb[:, 0:DC * C]),
                 reads=["pT"], writes=["xn1f", "bfB"])
            dump(f"xn1f_{c}", xn1f[:], ["xn1f"])
            if reached("proj"):
                return

            for tt in range(TT):
                if False:
                    pass
                def projqk(E_, tt=tt):
                    last = None
                    for dc in range(DC):
                        lt = xn1f[:, dc * C + tt * 128: dc * C + tt * 128 + 128]
                        for wsb, off in ((wq, 0), (wk, 1024)):
                            for hf in range(2):
                                last = E_.matmul(
                                    pA[:, off + hf * 512: off + hf * 512 + 512], lt,
                                    wsb[:, dc * 1024 + hf * 512: dc * 1024 + hf * 512 + 512],
                                    start=(dc == 0), stop=(dc == DC - 1))
                    return last
                S.op("tensor", projqk, reads=["xn1f", "wq", "wk"], writes=["pA"])

                def projv(E_, tt=tt):
                    last = None
                    for dc in range(DC):
                        lt = xn1f[:, dc * C + tt * 128: dc * C + tt * 128 + 128]
                        for hf in range(2):
                            last = E_.matmul(
                                pB[:, hf * 512: hf * 512 + 512], lt,
                                strA[:, dc * 1024 + hf * 512: dc * 1024 + hf * 512 + 512],
                                start=(dc == 0), stop=(dc == DC - 1))
                    return last
                PENDING_PROJV.append((c, tt, projv))
                if "xpos" in SKIP:
                    continue
                for name, poff, stab, ctab in (("q", 0, 1, 0), ("k", 1024, 3, 2)):
                    st_ap = tabs_ap(0, stab)[:, tt * 128:(tt + 1) * 128]
                    ct_ap = tabs_ap(0, ctab)[:, tt * 128:(tt + 1) * 128]

                    wb4 = workB[:, 0:1024].rearrange("p (h s e) -> p h s e", h=H, s=2)
                    st4lo = st_ap[:, 0:64].unsqueeze(1).unsqueeze(2).broadcast_to((128, H, 1, 64))
                    st4hi = st_ap[:, 64:128].unsqueeze(1).unsqueeze(2).broadcast_to((128, H, 1, 64))
                    pA4 = pA[:, poff:poff + 1024].rearrange("p (h s e) -> p h s e", h=H, s=2)

                    def swmul(E_, wb4=wb4, st4lo=st4lo, st4hi=st4hi, pA4=pA4):
                        E_.tensor_tensor(wb4[:, :, 0:1, :], pA4[:, :, 1:2, :], st4lo, ALU.mult)
                        return E_.tensor_tensor(wb4[:, :, 1:2, :], pA4[:, :, 0:1, :], st4hi, ALU.mult)
                    S.op("vector", swmul, reads=["pA", f"tab{stab}"], writes=["workB"])
                    ct3 = ct_ap.unsqueeze(1).broadcast_to((128, H, 128))
                    wc3 = workC[:, 0:1024].rearrange("p (h e) -> p h e", h=H)
                    pA3 = pA[:, poff:poff + 1024].rearrange("p (h e) -> p h e", h=H)

                    def cmul(E_, wc3=wc3, ct3=ct3, pA3=pA3):
                        return E_.tensor_tensor(wc3, pA3, ct3, ALU.mult)
                    S.op("vector", cmul, reads=["pA", f"tab{ctab}"], writes=["workC"])
                    if name == "q":
                        S.op("vector", lambda E_: E_.tensor_tensor(
                            workB[:, 0:1024], workB[:, 0:1024], workC[:, 0:1024], ALU.add),
                            reads=["workB", "workC"], writes=["workB"])

                        def qg_fn(E_, tt=tt):
                            last = None
                            for h in range(H):
                                last = E_.tensor_scalar(
                                    qgtm[:, tt * 1024 + h * 128: tt * 1024 + (h + 1) * 128],
                                    workB[:, h * 128:(h + 1) * 128],
                                    gic[:, tt * H + h: tt * H + h + 1], None, ALU.mult)
                            return last
                        S.op("vector", qg_fn2 if False else qg_fn, reads=["workB", "gic"], writes=[f"qgtm{tt}", "tmA"])
                    else:
                        S.op("vector", lambda E_, tt=tt: E_.tensor_tensor(
                            kxtm[:, tt * 1024:(tt + 1) * 1024], workB[:, 0:1024],
                            workC[:, 0:1024], ALU.add),
                            reads=["workB", "workC"], writes=[f"kxtm{tt}", "tmB"])

                        def kg_fn(E_, tt=tt):
                            last = None
                            for h in range(H):
                                last = E_.tensor_scalar(
                                    kgtm[:, tt * 1024 + h * 128: tt * 1024 + (h + 1) * 128],
                                    kxtm[:, tt * 1024 + h * 128: tt * 1024 + (h + 1) * 128],
                                    gdc[:, tt * H + h: tt * H + h + 1], None, ALU.mult)
                            return last
                        S.op("vector", kg_fn, reads=[f"kxtm{tt}", "gdc"], writes=[f"kgtm{tt}"])
            if reached("trans"):
                return
            transposes([(qgtm[:, tt * 1024 + h * 128: tt * 1024 + h * 128 + 128],
                         h * C + tt * 128) for h in range(H) for tt in range(TT)],
                       ["qgtm0", "qgtm1"],
                       mid=lambda mk: S.op("vector", lambda E_: E_.tensor_copy(
                           qgf[:, 0:1024], pTb[:, 0:1024]), reads=[mk], writes=["qgh0"]))
            S.op("vector", lambda E_: E_.tensor_copy(qgf[:, 1024:H * C], pTb[:, 1024:H * C]),
                 reads=["pT"], writes=["qgf"])
            transposes([(kxtm[:, tt * 1024 + h * 128: tt * 1024 + h * 128 + 128],
                         h * C + tt * 128) for h in range(H) for tt in range(TT)],
                       ["kxtm0", "kxtm1"],
                       mid=lambda mk: S.op("vector", lambda E_: E_.tensor_copy(
                           kxf[:, 0:1024], pTb[:, 0:1024]), reads=[mk], writes=["kxh0"]))
            S.op("vector", lambda E_: E_.tensor_copy(kxf[:, 1024:H * C], pTb[:, 1024:H * C]),
                 reads=["pT"], writes=["kxf"])
            dump(f"qgf_{c}", qgf[:], ["qgf"])
            dump(f"kxf_{c}", kxf[:], ["kxf"])
            dump(f"vtm_{c}", vtm[:], ["vtm0", "vtm1"])
            if part == 1:
                return
            attn_scores(c)

        def attn_scores(c):
            if reached("g"):
                return
            if reached("scores"):
                return
            for mt in range(TT):
                def sc_fn(E_, mt=mt, hr=range(H)):
                    last = None
                    for h in hr:
                        last = E_.matmul(pA[:, h * 256:(h + 1) * 256],
                                         kxf[:, h * C + mt * 128: h * C + mt * 128 + 128],
                                         qgf[:, h * C:(h + 1) * C],
                                         start=(h % 2 == 0), stop=(h % 2 == 1))
                    return last

                def mask_fn(E_, mt=mt, hr=range(H)):
                    last = None
                    for h in hr:
                        if mt == 0:
                            last = E_.scalar_tensor_tensor(
                                stm[:, h * 256:(h + 1) * 256], pA[:, h * 256:(h + 1) * 256],
                                gmc[:, h:h + 1], tri0[:], ALU.mult, ALU.mult)
                        else:
                            last = E_.scalar_tensor_tensor(
                                kxf[:, h * 128:(h + 1) * 128],
                                pA[:, h * 256 + 128:(h + 1) * 256],
                                gmc[:, H + h:H + h + 1], tri0[:, 0:128], ALU.mult, ALU.mult)
                    return last
                if mt == 0:
                    S.op("tensor", lambda E_: sc_fn(E_, 0, range(0, 4)),
                         reads=["qgh0", "kxh0"], writes=["pA", "pAh0"])
                    S.op("vector", lambda E_: mask_fn(E_, 0, range(0, 4)),
                         reads=["pAh0", "gmc", "tri0"], writes=["stm0", "tmA"])
                    S.op("tensor", lambda E_: sc_fn(E_, 0, range(4, H)),
                         reads=["qgf", "kxf"], writes=["pA"])
                    S.op("vector", lambda E_: mask_fn(E_, 0, range(4, H)),
                         reads=["pA", "gmc", "tri0"], writes=["stm0", "tmA"])
                else:
                    S.op("tensor", lambda E_: sc_fn(E_, 1, range(0, 4)),
                         reads=["qgh0", "kxh0"], writes=["pA", "pAh0"])
                    S.op("tensor", lambda E_: sc_fn(E_, 1, range(4, H)),
                         reads=["qgf", "kxf"], writes=["pA"])
                    S.op("vector", lambda E_: mask_fn(E_, 1, range(0, 4)),
                         reads=["pAh0", "gmc", "tri0"], writes=["sm1"])
                    S.op("vector", lambda E_: mask_fn(E_, 1, range(4, H)),
                         reads=["pA", "gmc", "tri0"], writes=["sm1"])

                PENDING_YIN.append((c, mt))

            if reached("y"):
                return

            return

        def attn_rest(c):
            xsl = xs_ap(c % 3)
            slk = f"xsl{c % 3}"
            xk = f"x{c}"
            if c + 1 < n_run:
                dma(xs_ap((c + 1) % 3).rearrange("p (tt d) -> p tt d", tt=TT),
                    x_in[(c + 1) * C:(c + 2) * C, :].rearrange("(tt t) d -> t tt d", t=128),
                    writes=[f"x{c+1}", f"xsl{(c + 1) % 3}"])
            for c2, tt, projv in [p for p in PENDING_PROJV if p[0] == c]:
                S.op("tensor", projv, reads=["xn1f", "strA"], writes=["pB"])
                S.op("vector", lambda E_, tt=tt: E_.tensor_copy(
                    vtm[:, tt * 1024:(tt + 1) * 1024], pB[:, 0:1024]),
                    reads=["pB"], writes=[f"vtm{tt}"])
            PENDING_PROJV[:] = [p for p in PENDING_PROJV if p[0] != c]
            dma(strA[:], wo_s, reads=["wo_s"], writes=["strA"])

            def gproj(E_):
                last = None
                for dc in range(DC):
                    for dco in range(DC):
                        last = E_.matmul(
                            pB[:, dco * 256:(dco + 1) * 256],
                            strB[:, dc * 1024 + dco * 128: dc * 1024 + dco * 128 + 128],
                            xn1f[:, dc * C:(dc + 1) * C],
                            start=(dc == 0 and dco % 2 == 0),
                            stop=(dc == DC - 1 and dco % 2 == 1))
                return last

            def emit_g():
                S.op("tensor", gproj, reads=["xn1f", "strB"], writes=["pB"])
                S.op("scalar", lambda E_: E_.activation(silg[:], pB[:], AF.Silu),
                     reads=["pB"], writes=["silg"])
                dma(strB[:].bitcast(FP8)[:, 0:DC * 2048], r1_s[0], reads=["w1_s"], writes=["strB"])
                dump(f"silg_{c}", silg[:], ["silg"])
            for c2, mt in [p for p in PENDING_YIN if p[0] == c]:
                def yin_fn(E_, mt=mt):
                    last = None
                    for h in range(H):
                        for it in (range(TT) if mt == 0 else range(1, TT)):
                            sap = (stm[:, h * 256 + it * 128: h * 256 + it * 128 + 128]
                                   if mt == 0 else kxf[:, h * 128:(h + 1) * 128])
                            last = E_.matmul(
                                pA[:, it * 1024 + h * 128: it * 1024 + (h + 1) * 128],
                                sap,
                                vtm[:, mt * 1024 + h * 128: mt * 1024 + (h + 1) * 128],
                                start=(mt == 0 and h % 4 == 0), stop=False)
                    return last
                S.op("tensor", yin_fn,
                     reads=[("stm0" if mt == 0 else "sm1"), "vtm0", "vtm1"], writes=["pA"])
            PENDING_YIN[:] = [p for p in PENDING_YIN if p[0] != c]

            def yint_fn(E_):
                last = None
                for h in range(H):
                    for it in range(TT):
                        last = E_.matmul(
                            pA[:, it * 1024 + h * 128: it * 1024 + (h + 1) * 128],
                            qgf[:, h * C + it * 128: h * C + it * 128 + 128],
                            st2b[:, h * 128:(h + 1) * 128],
                            start=False, stop=(h % 4 == 3))
                return last
            S.op("tensor", yint_fn, reads=["qgf", "st2b"], writes=["pA"])
            emit_g()

            if reached("state"):
                return

            if reached("gn"):
                return
            yhtm = workA[:].bitcast(BF16)
            for tt in range(TT):
                ya = yev[:]
                S.op("vector", lambda E_, tt=tt: E_.tensor_copy(yev[:], pA[:, tt * 1024:(tt + 1) * 1024]),
                     reads=["pA"], writes=["yev"])
                dump(f"yraw_{c}_{tt}", yev[:], ["yev"])
                rk = ["yev"]
                y3 = ya.rearrange("p (h v) -> p h v", h=H)
                gsl = gcols[:, 16 + tt * 8: 24 + tt * 8]
                qsl = gcols[:, 32 + tt * 8: 40 + tt * 8]
                msl = gcols[:, 48 + tt * 8: 56 + tt * 8]
                vsl = gcols[:, 64 + tt * 8: 72 + tt * 8]
                S.op("vector", lambda E_, y3=y3, gsl=gsl: E_.reduce_sum(
                    gsl.unsqueeze(2), y3, AX.X), reads=rk, writes=[f"gns{tt}"])
                S.op("scalar", lambda E_, ya=ya: E_.activation(workE[:], ya, AF.Square),
                     reads=rk, writes=["workE"])
                S.op("vector", lambda E_, qsl=qsl: E_.reduce_sum(
                    qsl.unsqueeze(2), workE[:].rearrange("p (h v) -> p h v", h=H), AX.X),
                    reads=["workE"], writes=[f"gnq{tt}"])
                S.op("vector", lambda E_, gsl=gsl, msl=msl: E_.tensor_scalar(
                    msl, gsl, 1.0 / E, None, ALU.mult), reads=[f"gns{tt}"], writes=[f"gnm{tt}"])
                S.op("vector", lambda E_, msl=msl, vsl=vsl: E_.scalar_tensor_tensor(
                    vsl, msl, 0.0, msl, ALU.add, ALU.mult),
                    reads=[f"gnm{tt}"], writes=[f"gnm2{tt}"])
                S.op("vector", lambda E_, qsl=qsl, vsl=vsl: E_.scalar_tensor_tensor(
                    vsl, qsl, 1.0 / E, vsl, ALU.mult, ALU.subtract),
                    reads=[f"gnq{tt}", f"gnm2{tt}"], writes=[f"gnv{tt}"])
                S.op("scalar", lambda E_, vsl=vsl: E_.activation(vsl, vsl, AF.Sqrt, bias=EPS),
                     reads=[f"gnv{tt}"], writes=[f"gnd{tt}"])
                S.op("vector", lambda E_, vsl=vsl: E_.reciprocal(vsl, vsl),
                     reads=[f"gnd{tt}"], writes=[f"gnr{tt}"])

                def gn_app(E_, ya=ya, msl=msl, vsl=vsl):
                    last = None
                    for h in range(H):
                        last = E_.tensor_scalar(
                            ya[:, h * E:(h + 1) * E], ya[:, h * E:(h + 1) * E],
                            msl[:, h:h + 1], vsl[:, h:h + 1], ALU.subtract, ALU.mult)
                    return last
                S.op("vector", gn_app, reads=[f"gnm{tt}", f"gnr{tt}", "yev"],
                     writes=["yev"])
                S.op("vector", lambda E_, tt=tt: E_.tensor_copy(
                    yhtm[:, tt * 1024:(tt + 1) * 1024], yev[:]),
                    reads=["yev"], writes=["workA"])
            dump(f"yev_{c}", workA[:], ["workA"])

            def u_fn(E_):
                last = None
                for h in range(H):
                    for tt in range(TT):
                        last = E_.matmul(
                            pA[:, h * 128:(h + 1) * 128],
                            kgtm[:, tt * 1024 + h * 128: tt * 1024 + (h + 1) * 128],
                            vtm[:, tt * 1024 + h * 128: tt * 1024 + (h + 1) * 128],
                            start=(tt == 0 and h % 4 == 0), stop=(tt == TT - 1 and h % 4 == 3))
                return last
            S.op("tensor", u_fn, reads=["kgtm0", "kgtm1", "vtm0", "vtm1"], writes=["pA"])

            def supd_a(E_):
                last = None
                for h in range(H):
                    last = E_.tensor_scalar(
                        workE[:, h * 128:(h + 1) * 128], st2f[:, h * 128:(h + 1) * 128],
                        svc[:, h:h + 1], None, ALU.mult)
                return last

            def supd_b(E_):
                last = None
                for h in range(H):
                    last = E_.scalar_tensor_tensor(
                        st2f[:, h * 128:(h + 1) * 128], pA[:, h * 128:(h + 1) * 128],
                        uvc[:, h:h + 1], workE[:, h * 128:(h + 1) * 128], ALU.mult, ALU.add)
                return last
            S.op("vector", supd_a, reads=["st2f", "svc"], writes=["workE"])
            S.op("vector", supd_b, reads=["pA", "workE", "uvc"], writes=["st2f"])
            S.op("vector", lambda E_: E_.tensor_copy(st2b[:], st2f[:]),
                 reads=["st2f"], writes=["st2b"])
            if reached("gate"):
                return
            def gnwb_a(E_, hh=0):
                last = None
                for h in range(hh * 4, hh * 4 + 4):
                    last = E_.scalar_tensor_tensor(
                        workB[:, h * C:(h + 1) * C], yhf[:, h * C:(h + 1) * C], gnwc[:, h:h + 1],
                        gnbc[:, h:h + 1].broadcast_to((128, C)), ALU.mult, ALU.add)
                return last

            def gnwb_b(E_, hh=0):
                last = None
                for h in range(hh * 4, hh * 4 + 4):
                    last = E_.tensor_tensor(
                        gat[:, h * C:(h + 1) * C], workB[:, h * C:(h + 1) * C],
                        silg[:, h * C:(h + 1) * C], ALU.mult)
                return last

            def yh_mid(mk):
                S.op("vector", lambda E_: E_.tensor_copy(
                    yhf[:, 0:1024], pTb[:, 0:1024]), reads=[mk], writes=["yhfh0"])
                S.op("vector", lambda E_: gnwb_a(E_, 0), reads=["yhfh0", "gnwc", "gnbc"],
                     writes=["workB", "wBh0"])
                S.op("vector", lambda E_: gnwb_b(E_, 0), reads=["wBh0", "silg"],
                     writes=["gath0"])
            transposes([(yhtm[:, tt * 1024 + h * 128: tt * 1024 + h * 128 + 128],
                         h * C + tt * 128) for h in range(H) for tt in range(TT)],
                       ["workA"], mid=yh_mid)
            S.op("vector", lambda E_: E_.tensor_copy(yhf[:, 1024:H * C], pTb[:, 1024:H * C]),
                 reads=["pT"], writes=["yhf", "bfA"])
            S.op("vector", lambda E_: gnwb_a(E_, 1), reads=["yhf", "gnwc", "gnbc"],
                 writes=["workB"])
            S.op("vector", lambda E_: gnwb_b(E_, 1), reads=["workB", "silg"],
                 writes=["gat", "bfB"])
            dump(f"gat_{c}", gat[:], ["gat"])

            if reached("wo"):
                return

            def wo_fn(E_, tt=0):
                last = None
                for dc in range(DC):
                    lt = gat[:, dc * C + tt * 128: dc * C + tt * 128 + 128]
                    for hf in range(2):
                        last = E_.matmul(
                            pB[:, tt * 1024 + hf * 512: tt * 1024 + hf * 512 + 512], lt,
                            strA[:, dc * 1024 + hf * 512: dc * 1024 + hf * 512 + 512],
                            start=(dc == 0), stop=(dc == DC - 1))
                return last
            S.op("tensor", lambda E_: wo_fn(E_, 0), reads=["gat", "strA"],
                 writes=["pB", "pBt0"])
            S.op("vector", lambda E_: E_.tensor_copy(workB[:, 0:1024], pB[:, 0:1024]),
                 reads=["pBt0"], writes=["workB", "wBt0"])
            S.op("tensor", lambda E_: wo_fn(E_, 1), reads=["gat", "strA"], writes=["pB"])
            dma(strA[:].bitcast(FP8)[:, 0:DC * 2048], r1_s[1], reads=["w1_s"], writes=["strA"])
            if reached("ln2"):
                return
            S.op("vector", lambda E_: E_.tensor_copy(workB[:, 1024:2048], pB[:, 1024:2048]),
                 reads=["pB"], writes=["workB"])
            ss = "A"
            ln_stats("l2", lambda tt: workB[:, tt * 1024:(tt + 1) * 1024], ["wBt0"], ss="A",
                     readkeys2=["workB"])
            for tt in range(TT):
                S.op("vector", lambda E_, tt=tt: E_.tensor_scalar(
                    workC[:, 0:1024], workB[:, tt * 1024:(tt + 1) * 1024],
                    col(f"ln{ss}m{tt}"), col(f"ln{ss}r{tt}"), ALU.subtract, ALU.mult),
                    reads=["workB", f"ln{ss}m{tt}", f"ln{ss}r{tt}"], writes=["workC"])
                S.op("vector", lambda E_, tt=tt: E_.tensor_tensor(
                    xsl[:, tt * 1024:(tt + 1) * 1024], xsl[:, tt * 1024:(tt + 1) * 1024],
                    workC[:, 0:1024], ALU.add),
                    reads=["workC", xk, slk], writes=[f"xp{c}" if tt == TT - 1 else f"xpp{c}", slk])
            dump(f"xp_{c}", xsl, [f"xp{c}", slk])

        def mm1_chunk(c):
            xsl = xs_ap(c % 3)
            slk = f"xsl{c % 3}"
            hb = hbf_ap(c % 2)
            hslk = f"hslot{c % 2}"
            hk = f"h{c}"
            if reached("a1"):
                return
            ss = "B"
            ln_stats("l3", lambda tt: xsl[:, tt * 1024:(tt + 1) * 1024], [f"xp{c}", slk], ss="B")
            a1tm = workC[:].bitcast(BF16)
            for tt in range(TT):
                S.op("vector", lambda E_, tt=tt: E_.tensor_scalar(
                    workA[:, 0:1024], xsl[:, tt * 1024:(tt + 1) * 1024],
                    col(f"ln{ss}m{tt}"), col(f"ln{ss}r{tt}"), ALU.subtract, ALU.mult),
                    reads=[f"xp{c}", slk, f"ln{ss}m{tt}", f"ln{ss}r{tt}"], writes=["workA"])
                if "q1max" in SKIP:
                    S.op("vector", lambda E_, tt=tt: E_.tensor_copy(
                        a1tm[:, tt * 1024:(tt + 1) * 1024], workA[:, 0:1024]),
                        reads=["workA"], writes=["workC"])
                    continue
                S.op("vector", lambda E_, tt=tt: E_.tensor_reduce(
                    col(f"q1x{tt}"), workA[:, 0:1024], AX.X, ALU.max, apply_absolute_value=True),
                    reads=["workA"], writes=[f"q1x{tt}"])
                S.op("vector", lambda E_, tt=tt: E_.tensor_scalar(
                    col(f"q1c{tt}"), col(f"q1x{tt}"), 1e-5, None, ALU.max),
                    reads=[f"q1x{tt}"], writes=[f"q1c{tt}"])
                S.op("vector", lambda E_, tt=tt: E_.reciprocal(
                    col(f"q1r{tt}"), col(f"q1c{tt}")),
                    reads=[f"q1c{tt}"], writes=[f"q1r{tt}"])
                S.op("vector", lambda E_, tt=tt: E_.tensor_scalar(
                    workB[:, 0:1024], workA[:, 0:1024], col(f"q1r{tt}"), 127.0,
                    ALU.mult, ALU.mult),
                    reads=["workA", f"q1r{tt}"], writes=["workB"])
                if "q1round" in SKIP:
                    S.op("vector", lambda E_, tt=tt: E_.tensor_copy(
                        a1tm[:, tt * 1024:(tt + 1) * 1024], workB[:, 0:1024]),
                        reads=["workB"], writes=["workC"])
                    continue
                i32v = workE[:].bitcast(I32)
                S.op("vector", lambda E_, i32v=i32v: E_.tensor_scalar(
                    i32v, workB[:, 0:1024], 16384.0, None, ALU.add),
                    reads=["workB"], writes=["workE"])
                S.op("vector", lambda E_, i32v=i32v: E_.scalar_tensor_tensor(
                    workA[:, 0:1024], workB[:, 0:1024], 16383.5, i32v, ALU.add, ALU.is_ge),
                    reads=["workB", "workE"], writes=["workA"])
                S.op("vector", lambda E_, i32v=i32v: E_.tensor_tensor(
                    workB[:, 0:1024], workA[:, 0:1024], i32v, ALU.add),
                    reads=["workA", "workE"], writes=["workB"])
                S.op("vector", lambda E_, tt=tt: E_.tensor_scalar(
                    col(f"q1i{tt}"), col(f"q1c{tt}"), 1.0 / 127, col("w1isw"), ALU.mult, ALU.mult),
                    reads=[f"q1c{tt}", "w1isw"], writes=[f"q1i{tt}"])
                S.op("vector", lambda E_, tt=tt: E_.tensor_scalar(
                    a1tm[:, tt * 1024:(tt + 1) * 1024], workB[:, 0:1024],
                    -16384.0, col(f"q1i{tt}"), ALU.add, ALU.mult),
                    reads=["workB", f"q1i{tt}"], writes=["workC"])
            transposes([(a1tm[:, tt * 1024 + dc * 128: tt * 1024 + dc * 128 + 128],
                         dc * C + tt * 128) for tt in range(TT) for dc in range(DC)],
                       ["workC"])
            S.op("vector", lambda E_: E_.tensor_copy(a1f[:], pTb[:, 0:DC * C]),
                 reads=["pT"], writes=["a1f", "bfA"])
            dump(f"a1f_{c}", a1f[:], ["a1f"])
            if reached("mm1"):
                return
            for q in range(4):
                sbuf_w = strB if q < 2 else strA
                wkey = "strB" if q < 2 else "strA"
                ptgt, pkey = (pA, "pA") if q % 2 == 0 else (pB, "pB")

                def mm1_fn(E_, q=q, sbuf_w=sbuf_w, ptgt=ptgt):
                    last = None
                    w8 = sbuf_w[:].bitcast(FP8)
                    for j8 in range(8):
                        jloc = (q * 8 + j8) % 16
                        for dc in range(DC):
                            last = E_.matmul(
                                ptgt[:, j8 * 256:(j8 + 1) * 256],
                                w8[:, dc * 2048 + jloc * 128: dc * 2048 + (jloc + 1) * 128],
                                a1f[:, dc * C:(dc + 1) * C],
                                start=(dc == 0 and j8 % 2 == 0),
                                stop=(dc == DC - 1 and j8 % 2 == 1))
                    return last
                S.op("tensor", mm1_fn, reads=["a1f", wkey], writes=[pkey])
                hdst = hb.rearrange("p (j s) -> p j s", s=HS)[:, q * 8:(q + 1) * 8, 1:257]
                S.op("scalar", lambda E_, ptgt=ptgt, hdst=hdst: E_.copy(
                    hdst, ptgt[:].rearrange("p (j t) -> p j t", t=256)),
                    reads=[pkey], writes=[f"{hk}q{q}", hslk])
                if c >= 1:
                    pb3 = hbf_ap((c - 1) % 2).rearrange("p (j s) -> p j s", s=HS)
                    S.op("vector", lambda E_, q=q, pb3=pb3: E_.tensor_copy(
                        pb3[:, q * 8:(q + 1) * 8, 257:258],
                        hb.rearrange("p (j s) -> p j s", s=HS)[:, q * 8:(q + 1) * 8, 1:2]),
                        reads=[f"{hk}q{q}", hslk, f"hslot{(c - 1) % 2}"],
                        writes=[f"h{c-1}hr{q}"])
                if q == 1:
                    dma(strB[:].bitcast(FP8)[:, 0:16 * D], r2_s[0], reads=["w2_s"], writes=["strB"])
                if q == 3:
                    dma(strA[:].bitcast(FP8)[:, 0:16 * D], r2_s[1], reads=["w2_s"], writes=["strA"])
            hsrc = hb.rearrange("p (j s) -> p j s", s=HS)[:, :, 256:257]
            S.op("vector", lambda E_, hsrc=hsrc: E_.tensor_copy(
                halo_ap(c % 3).unsqueeze(2), hsrc),
                reads=[f"{hk}q{q}" for q in range(4)] + [hslk], writes=[f"halo{c % 3}"])
            dump(f"h_{c}", hb, [f"{hk}q{q}" for q in range(4)] + [hslk])

        def ffn_conv(c):
            hb = hbf_ap(c % 2)
            hslk = f"hslot{c % 2}"
            hk = f"h{c}"
            h3 = hb.rearrange("p (j s) -> p j s", s=HS)
            xsl = xs_ap(c % 3)
            slk = f"xsl{c % 3}"
            if reached("conv"):
                return
            hrd = [f"{hk}q{q}" for q in range(4)]
            S.op("vector", lambda E_: E_.tensor_copy(
                h3[:, :, 0:1], halo_ap((c - 1) % 3).unsqueeze(2)),
                reads=[f"halo{(c - 1) % 3}"] + hrd + [hslk], writes=[hk + "hl", hslk])
            if c + 1 >= n_run:
                S.op("vector", lambda E_: E_.memset(h3[:, :, 257:258], 0.0),
                     writes=[f"{hk}hr{q}" for q in range(4)] + [hslk])
            hsk = []
            for q in range(4):
                j0 = q * 8
                def hq(sh, j0=j0):
                    return h3[:, j0:j0 + 8, sh:sh + 256]
                wB3 = workB[:].rearrange("p (j t) -> p j t", t=256)
                rks = [hk + "hl", f"{hk}hr{q}"] + hrd

                def tap0(E_, hq=hq, wB3=wB3, j0=j0):
                    last = None
                    for j8 in range(8):
                        jc = j0 + j8
                        last = E_.scalar_tensor_tensor(
                            wB3[:, j8], hq(0)[:, j8], cwc[:, jc * 3 + 0:jc * 3 + 1],
                            cbc[:, jc:jc + 1].broadcast_to((128, 256)), ALU.mult, ALU.add)
                    return last
                S.op("vector", tap0, reads=rks + [hslk, "cwc", "cbc"], writes=["workB"])

                def tap2(E_, hq=hq, wB3=wB3, j0=j0):
                    last = None
                    for j8 in range(8):
                        jc = j0 + j8
                        last = E_.scalar_tensor_tensor(
                            wB3[:, j8], hq(2)[:, j8], cwc[:, jc * 3 + 2:jc * 3 + 3],
                            wB3[:, j8], ALU.mult, ALU.add)
                    return last
                S.op("vector", tap2, reads=rks + [hslk, "cwc", "workB"], writes=["workB"])

                def cen_fn(E_, hq=hq, wB3=wB3, j0=j0):
                    last = None
                    for j8 in range(8):
                        jc = j0 + j8
                        last = E_.scalar_tensor_tensor(
                            wB3[:, j8], hq(1)[:, j8], cwc[:, jc * 3 + 1:jc * 3 + 2],
                            wB3[:, j8], ALU.mult, ALU.add)
                    return last
                S.op("vector", cen_fn, reads=["workB", "cwc", hslk] + rks, writes=["workB"])
                S.op("scalar", lambda E_, q=q, h3=h3: E_.activation(
                    h3[:, q * 8:(q + 1) * 8, 1:257],
                    workB[:].rearrange("p (j t) -> p j t", t=256), AF.Silu),
                    reads=["workB"], writes=[f"hs{c}q{q}", hslk])
                hsk.append(f"hs{c}q{q}")
            dump(f"hs_{c}", hb, hsk + [hslk])

        def ffn_tail(c):
            if reached("q2"):
                return
            hb = hbf_ap(c % 2)
            hslk = f"hslot{c % 2}"
            hk = f"h{c}"
            h3 = hb.rearrange("p (j s) -> p j s", s=HS)
            xsl = xs_ap(c % 3)
            slk = f"xsl{c % 3}"
            hsk = [f"hs{c}q{q}" for q in range(4)]
            for q in range(4):
                S.op("vector", lambda E_, q=q, h3=h3: E_.tensor_reduce(
                    workE[:, q * 256:(q + 1) * 256].unsqueeze(2),
                    h3[:, q * 8:(q + 1) * 8, 1:257].transpose([0, 2, 1]), AX.X,
                    ALU.max, apply_absolute_value=True),
                    reads=[f"hs{c}q{q}", hslk], writes=[f"mxp{q}", "workE"])
            S.op("vector", lambda E_: E_.tensor_tensor(
                workE[:, 0:256], workE[:, 0:256], workE[:, 256:512], ALU.max),
                reads=["mxp0", "mxp1", "workE"], writes=["mxp0", "workE"])
            S.op("vector", lambda E_: E_.tensor_tensor(
                workE[:, 512:768], workE[:, 512:768], workE[:, 768:1024], ALU.max),
                reads=["mxp2", "mxp3", "workE"], writes=["mxp2", "workE"])
            S.op("vector", lambda E_: E_.tensor_tensor(
                workE[:, 0:256], workE[:, 0:256], workE[:, 512:768], ALU.max),
                reads=["mxp0", "mxp2", "workE"], writes=["mxq", "workE"])
            for i in range(2):
                S.op("tensor", lambda E_, i=i: E_.transpose(
                    pB[:, i * 128:(i + 1) * 128], workE[:, i * 128:(i + 1) * 128], identf[:]),
                    reads=["mxq", "workE", "identf"], writes=["pB"])
            for tt in range(TT):
                S.op("vector", lambda E_, tt=tt: E_.tensor_reduce(
                    col(f"q2x{tt}"), pB[:, tt * 128:(tt + 1) * 128], AX.X, ALU.max,
                    apply_absolute_value=True), reads=["pB"], writes=[f"q2x{tt}"])
                S.op("vector", lambda E_, tt=tt: E_.tensor_scalar(
                    col(f"q2c{tt}"), col(f"q2x{tt}"), 1e-5, None, ALU.max),
                    reads=[f"q2x{tt}"], writes=[f"q2c{tt}"])
                S.op("vector", lambda E_, tt=tt: E_.reciprocal(
                    col(f"q2r{tt}"), col(f"q2c{tt}")), reads=[f"q2c{tt}"], writes=[f"q2r{tt}"])
                S.op("vector", lambda E_, tt=tt: E_.tensor_scalar(
                    col(f"q2s{tt}"), col(f"q2r{tt}"), 127.0, None, ALU.mult),
                    reads=[f"q2r{tt}"], writes=[f"q2s{tt}"])
                S.op("vector", lambda E_, tt=tt: E_.tensor_scalar(
                    col(f"q2i{tt}"), col(f"q2c{tt}"), 1.0 / 127, col("w2isw"),
                    ALU.mult, ALU.mult), reads=[f"q2c{tt}", "w2isw"], writes=[f"q2i{tt}"])
            for ci, nm in enumerate(("q2s0", "q2s1", "q2i0", "q2i1")):
                S.op("tensor", lambda E_, ci=ci, nm=nm: E_.transpose(
                    pB[0:1, 256 + ci * 128:256 + (ci + 1) * 128], col(nm), identf[:]),
                    reads=[nm, "identf", "pB"], writes=["pB"])
            for ci in range(4):
                S.op("vector", lambda E_, ci=ci: E_.tensor_copy(
                    rowsb[0:1, ci * 128:(ci + 1) * 128],
                    pB[0:1, 256 + ci * 128:256 + (ci + 1) * 128]),
                    reads=["pB"], writes=[f"row{ci}", "workE"])
            S.op("tensor", lambda E_: E_.matmul(pB[:, 1024:1280], onesr[:], rowsb[0:1, 0:256],
                                                start=True, stop=True),
                 reads=["row0", "row1", "workE", "onesr"], writes=["pB"])
            S.op("tensor", lambda E_: E_.matmul(pB[:, 1280:1536], onesr[:], rowsb[0:1, 256:512],
                                                start=True, stop=True),
                 reads=["row2", "row3", "workE", "onesr"], writes=["pB"])
            S.op("vector", lambda E_: E_.tensor_copy(s2bc[:], pB[:, 1024:1280]),
                 reads=["pB"], writes=["s2bc"])
            S.op("vector", lambda E_: E_.tensor_copy(is2bc[:], pB[:, 1280:1536]),
                 reads=["pB"], writes=["is2bc"])
            for q in range(4):
                hq2 = h3[:, q * 8:(q + 1) * 8, 1:257]
                s2b3 = s2bc[:].unsqueeze(1).broadcast_to((128, 8, 256))
                wA3 = workA[:].rearrange("p (j t) -> p j t", t=256)
                i32v = workC[:].bitcast(I32)
                S.op("vector", lambda E_, hq2=hq2, s2b3=s2b3, wA3=wA3: E_.tensor_tensor(
                    wA3, hq2, s2b3, ALU.mult), reads=hsk + [hslk, "s2bc"], writes=["workA"])
                S.op("vector", lambda E_, i32v=i32v: E_.tensor_scalar(
                    i32v, workA[:], 16384.0, None, ALU.add),
                    reads=["workA"], writes=["workC"])
                S.op("vector", lambda E_, i32v=i32v: E_.scalar_tensor_tensor(
                    workB[:], workA[:], 16383.5, i32v, ALU.add, ALU.is_ge),
                    reads=["workA", "workC"], writes=["workB"])
                S.op("vector", lambda E_, i32v=i32v: E_.scalar_tensor_tensor(
                    r2q[:], workB[:], -16384.0, i32v, ALU.add, ALU.add),
                    reads=["workB", "workC"], writes=["r2q", "tmB"])
                sbuf_w = strB if q < 2 else strA
                wkey = "strB" if q < 2 else "strA"

                def mm2_fn(E_, q=q, sbuf_w=sbuf_w):
                    last = None
                    w8 = sbuf_w[:].bitcast(FP8)
                    for j8 in range(8):
                        jloc = (q * 8 + j8) % 16
                        for dc in range(DC):
                            last = E_.matmul(
                                pB[:, dc * 256:(dc + 1) * 256],
                                w8[:, jloc * D + dc * 128: jloc * D + (dc + 1) * 128],
                                r2q[:, j8 * 256:(j8 + 1) * 256],
                                start=(q == 0 and j8 == 0 and dc % 2 == 0),
                                stop=(q == 3 and j8 == 7 and dc % 2 == 1))
                    return last
                S.op("tensor", mm2_fn, reads=["r2q", wkey], writes=["pB"])
            o2tm = workA[:].bitcast(BF16)
            is23 = is2bc[:].unsqueeze(1).broadcast_to((128, 4, 256))
            for hf in range(2):
                S.op("vector", lambda E_, hf=hf, is23=is23: E_.tensor_tensor(
                    yev[:].rearrange("p (d t) -> p d t", t=256),
                    pB[:, hf * 1024:(hf + 1) * 1024].rearrange("p (d t) -> p d t", t=256),
                    is23, ALU.mult),
                    reads=["pB", "is2bc"], writes=["yev"])
                S.op("vector", lambda E_, hf=hf: E_.tensor_copy(
                    o2tm[:, hf * 1024:(hf + 1) * 1024], yev[:]),
                    reads=["yev"], writes=["workA"])
            transposes([(o2tm[:, dc * 256 + tt * 128: dc * 256 + tt * 128 + 128],
                         tt * 1024 + dc * 128) for tt in range(TT) for dc in range(DC)],
                       ["workA"], bank="B",
                       mid=lambda mk: S.op("vector", lambda E_: E_.tensor_copy(
                           workB[:].bitcast(BF16)[:, 0:1024], pTbB[:, 0:1024]),
                           reads=[mk], writes=["o2h0"]))
            o2 = workB[:].bitcast(BF16)
            S.op("vector", lambda E_: E_.tensor_copy(o2[:, 1024:2048], pTbB[:, 1024:2048]),
                 reads=["pB"], writes=["workB"])
            dump(f"o2f_{c}", o2[:, 0:2048], ["workB"])
            ss = "C"
            ln_stats("l4", lambda tt: o2[:, tt * 1024:(tt + 1) * 1024], ["o2h0"], ss="C",
                     readkeys2=["workB"])
            dump(f"xpt_{c}", xsl, [f"xp{c}", slk])
            for tt in range(TT):
                S.op("vector", lambda E_, tt=tt: E_.tensor_scalar(
                    workC[:, 0:1024], o2[:, tt * 1024:(tt + 1) * 1024],
                    col(f"ln{ss}m{tt}"), col(f"ln{ss}r{tt}"), ALU.subtract, ALU.mult),
                    reads=["workB", f"ln{ss}m{tt}", f"ln{ss}r{tt}"], writes=["workC"])
                S.op("vector", lambda E_, tt=tt: E_.tensor_tensor(
                    xsl[:, tt * 1024:(tt + 1) * 1024], xsl[:, tt * 1024:(tt + 1) * 1024],
                    workC[:, 0:1024], ALU.add),
                    reads=["workC", f"xp{c}", slk], writes=[f"xf{c}t{tt}"])
                dma(out_d[c * C + tt * 128: c * C + (tt + 1) * 128, :],
                    xsl[:, tt * 1024:(tt + 1) * 1024],
                    reads=[f"xf{c}t{tt}", slk], writes=[f"store{c}t{tt}"])
                dump(f"xf_{c}_{tt}", xsl[:, tt * 1024:(tt + 1) * 1024],
                     [f"xf{c}t{tt}", slk])
            if c + 2 < n_run:
                dma(strA[:], wv_s, reads=["wv_s"], writes=["strA"])
                dma(strB[:], wg_s, reads=["wg_s"], writes=["strB"])

        attn_front(0)
        for c in range(n_run):
            attn_rest(c)
            mm1_chunk(c)
            if c >= 1:
                ffn_conv(c - 1)
            if c >= 1:
                ffn_tail(c - 1)
            if c + 1 < n_run:
                attn_front(c + 1, part=1)
                attn_front(c + 1, part=2)
            if c == 0 and n_run > 1:
                dma(strA[:], wv_s, reads=["wv_s"], writes=["strA"])
                dma(strB[:], wg_s, reads=["wg_s"], writes=["strB"])
        ffn_conv(n_run - 1)
        ffn_tail(n_run - 1)
        fcounts = [(s, S.count[s]) for s in SEMS if (s.startswith("dsync") or s == "gdma") and S.count[s]]

        def fence(E_, fcounts=fcounts):
            for s, v in fcounts:
                E_.wait_ge(sems[s], v)
            return None
        S.op("vector", fence, inc=0)

        S.emit_all(nc, sems)
    return nc


def kernel(**inputs):
    in_maps = host_prep(inputs)
    nc = build_program(NCH)
    res = run_bass_kernel_spmd(nc, in_maps, core_ids=list(range(NCORE)))
    out = np.stack([np.asarray(r["out"]) for r in res.results], axis=0)
    return out.astype(np.float32)
